# version 02
# baseline (speedup 1.0000x reference)
import numpy as np
from contextlib import ExitStack

import concourse.bass as bass
import concourse.mybir as mybir
from concourse.bass_utils import run_bass_kernel_spmd

F32 = mybir.dt.float32
BF16 = mybir.dt.bfloat16
AF = mybir.ActivationFunctionType
ALU = mybir.AluOpType
AX = mybir.AxisListType

B, L, D = 4, 4096, 1024
H, DK, DV = 6, 128, 256
KD3, VD3 = 384, 768
C = 128
NCH = L // C
FH, RK = 4096, 16
EPS = 1e-5
TT = 512

STREAMS = ("tensor", "vector", "scalar", "gpsimd", "sync")


class Buf:
    __slots__ = ("name", "writer", "readers")

    def __init__(self, name):
        self.name = name
        self.writer = None
        self.readers = []


class Emitter:
    def __init__(self):
        self.ops = []
        self.cnt = {(s, k): 0 for s in STREAMS for k in ("c", "d")}

    def _op(self, stream, kind, fn, reads=(), writes=()):
        deps = []
        for b in reads:
            if b.writer is not None:
                deps.append(b.writer)
        for b in writes:
            if b.writer is not None:
                deps.append(b.writer)
            deps.extend(b.readers)
        self.cnt[(stream, kind)] += 1
        tok = (stream, kind, self.cnt[(stream, kind)])
        if stream == "tensor" or kind == "d":
            waits = [d for d in deps if (d[0], d[1]) != (stream, kind)]
        else:
            waits = list(deps)
        self.ops.append((stream, kind, fn, waits))
        for b in reads:
            b.readers.append(tok)
        for b in writes:
            b.writer = tok
            b.readers = []
        return tok

    def op(self, stream, fn, reads=(), writes=()):
        return self._op(stream, "c", fn, reads, writes)

    def dma(self, stream, fn, reads=(), writes=()):
        assert stream in ("sync", "scalar", "gpsimd")
        return self._op(stream, "d", fn, reads, writes)

    def emit(self, nc, block, sems):
        streams = {s: [(k, fn, w) for (ss, k, fn, w) in self.ops if ss == s]
                   for s in STREAMS}

        def run_stream(eng, s):
            hw = {}
            ndma = 0
            for kind, fn, waits in streams[s]:
                if kind == "d" and ndma:
                    if hw.get((s, "d"), 0) < 16 * ndma:
                        hw[(s, "d")] = 16 * ndma
                        eng.wait_ge(sems[(s, "d")], 16 * ndma)
                for (ws, wk, wc) in waits:
                    val = wc * (16 if wk == "d" else 1)
                    if hw.get((ws, wk), 0) < val:
                        hw[(ws, wk)] = val
                        eng.wait_ge(sems[(ws, wk)], val)
                if kind == "d":
                    ndma += 1
                fn(eng).then_inc(sems[(s, kind)], 16 if kind == "d" else 1)
            if ndma:
                eng.wait_ge(sems[(s, "d")], 16 * ndma)

        @block.tensor
        def _(eng):
            run_stream(eng, "tensor")

        @block.vector
        def _(eng):
            run_stream(eng, "vector")

        @block.scalar
        def _(eng):
            run_stream(eng, "scalar")

        @block.gpsimd
        def _(eng):
            run_stream(eng, "gpsimd")

        @block.sync
        def _(eng):
            run_stream(eng, "sync")


class Ctx:
    def __init__(self, nc, em, st):
        self.nc = nc
        self.em = em
        self.st = st

    def sb(self, name, shape, dtype):
        t = self.st.enter_context(self.nc.sbuf_tensor(name, list(shape), dtype))
        return t, Buf(name)

    def ps(self, name, shape, dtype=F32):
        t = self.st.enter_context(self.nc.psum_tensor(name, list(shape), dtype))
        return t, Buf(name)


def host_constants():
    sel_last = np.zeros((128, 128), np.float32)
    sel_last[127, :] = 1.0
    sel3 = np.zeros((4, 384), np.float32)
    for h in range(3):
        sel3[h, 128 * h:128 * (h + 1)] = 1.0
    return dict(
        c_sel3=sel3,
        c_ident=np.eye(128, dtype=np.float32),
        c_triu=np.triu(np.ones((128, 128), np.float32)),
        c_trils=np.tril(np.ones((128, 128), np.float32), -1),
        c_sel_last=sel_last,
        c_onescol=np.ones((128, 1), np.float32),
        c_onesrow=np.ones((1, 128), np.float32),
    )


def load_constants(cx, em):
    nc = cx.nc
    cc = {}
    specs = [("c_sel3", (4, 384), F32),
             ("c_ident", (128, 128), BF16), ("c_triu", (128, 128), F32),
             ("c_trils", (128, 128), BF16), ("c_sel_last", (128, 128), F32),
             ("c_onescol", (128, 1), BF16), ("c_onesrow", (1, 128), BF16)]
    stg, stg_b = cx.sb("cstg", (128, 128), F32)
    for name, shape, dt_sb in specs:
        dr = nc.dram_tensor(name, list(shape), F32, kind="ExternalInput").ap()
        t, b = cx.sb("s" + name, shape, dt_sb)
        if dt_sb == F32:
            em.dma("sync", lambda e, t=t, dr=dr: e.dma_start(out=t[:], in_=dr), writes=[b])
        else:
            em.dma("sync", lambda e, dr=dr, shape=shape: e.dma_start(
                out=stg[0:shape[0], 0:shape[1]], in_=dr), writes=[stg_b])
            em.op("vector", lambda e, t=t, shape=shape: e.tensor_copy(
                out=t[:], in_=stg[0:shape[0], 0:shape[1]]), reads=[stg_b], writes=[b])
        cc[name] = (t, b)
    t, b = cx.sb("s_triu_bf", (128, 128), BF16)
    em.op("vector", lambda e, t=t: e.tensor_copy(out=t[:], in_=cc["c_triu"][0][:]),
          reads=[cc["c_triu"][1]], writes=[b])
    cc["c_triu_bf"] = (t, b)
    t, b = cx.sb("s_ident_f", (128, 128), F32)
    em.op("vector", lambda e, t=t: e.tensor_copy(out=t[:], in_=cc["c_ident"][0][:]),
          reads=[cc["c_ident"][1]], writes=[b])
    cc["c_ident_f"] = (t, b)
    cb, cb_b = cx.sb("cbias", (128, 4), F32)
    for i, val in enumerate([1e-5, 1e-12, -0.5 * float(np.log(128.0)), 1.0]):
        em.op("vector", lambda e, i=i, val=val: e.memset(cb[:, i:i + 1], val), writes=[cb_b])
    cc["cbias"] = (cb, cb_b)
    return cc



def build_attn_program(n_chunks=NCH, debug=False):
    nc = bass.Bass("TRN2", target_bir_lowering=False)
    em = Emitter()
    with ExitStack() as st:
        cx = Ctx(nc, em, st)
        cx.debug = debug
        P = n_chunks * C
        x_in = nc.dram_tensor("x", [P, D], F32, kind="ExternalInput").ap()
        wq_in = nc.dram_tensor("wq", [D, KD3], F32, kind="ExternalInput").ap()
        wk_in = nc.dram_tensor("wk", [D, KD3], F32, kind="ExternalInput").ap()
        wv_in = nc.dram_tensor("wv", [D, VD3], F32, kind="ExternalInput").ap()
        wg_in = nc.dram_tensor("wg", [D, VD3], F32, kind="ExternalInput").ap()
        wba_in = nc.dram_tensor("wba", [D, 8], F32, kind="ExternalInput").ap()
        cq_in = nc.dram_tensor("convq", [KD3, 4], F32, kind="ExternalInput").ap()
        ck_in = nc.dram_tensor("convk", [KD3, 4], F32, kind="ExternalInput").ap()
        cv_in = nc.dram_tensor("convv", [VD3, 4], F32, kind="ExternalInput").ap()
        aldt_in = nc.dram_tensor("aldt", [1, 8], F32, kind="ExternalInput").ap()
        og_out = nc.dram_tensor("og", [VD3, P], F32, kind="ExternalOutput").ap()
        if debug:
            cx.dbg_out = nc.dram_tensor("dbg", [128, 4096], F32, kind="ExternalOutput").ap()

        cc = load_constants(cx, em)
        build_attn_body(cx, em, cc, x_in, wq_in, wk_in, wv_in, wg_in, wba_in,
                        cq_in, ck_in, cv_in, aldt_in, og_out, n_chunks)
        sems = {}
        with ExitStack() as st2:
            for s in STREAMS:
                for k in ("c", "d"):
                    sems[(s, k)] = st2.enter_context(nc.semaphore(f"sem_{s}_{k}"))
            block = st2.enter_context(nc.Block())
            em.emit(nc, block, sems)
    return nc


def build_attn_body(cx, em, cc, x_in, wq_in, wk_in, wv_in, wg_in, wba_in,
                    cq_in, ck_in, cv_in, aldt_in, og_out, n_chunks):
    nc = cx.nc
    ident, ident_b = cc["c_ident"]
    identf, identf_b = cc["c_ident_f"]
    triu, triu_b = cc["c_triu"]
    triubf, triubf_b = cc["c_triu_bf"]
    trils, trils_b = cc["c_trils"]
    sel_last, sel_last_b = cc["c_sel_last"]
    sel3, sel3_b = cc["c_sel3"]
    cb, cb_b = cc["cbias"]
    onescol, onescol_b = cc["c_onescol"]
    onesrow, onesrow_b = cc["c_onesrow"]
    P = n_chunks * C
    NT = P // TT

    def mm(out_ap, lhsT_ap, rhs_ap, reads, wbuf, start=True, stop=True):
        em.op("tensor", lambda e: e.matmul(out=out_ap, lhsT=lhsT_ap, rhs=rhs_ap,
                                           start=start, stop=stop),
              reads=list(reads), writes=[wbuf])

    def tr(out_ap, in_ap, reads, wbuf, idt=None, idb=None, p=128):
        it = ident if idt is None else idt
        ib = ident_b if idb is None else idb
        em.op("tensor", lambda e: e.transpose(out=out_ap, in_=in_ap, identity=it[0:p, 0:p]),
              reads=list(reads) + [ib], writes=[wbuf])

    def load_weight(name, dr, rows, cols):
        nch = rows // 128
        stg, stg_b = cx.sb(name + "_stg", (128, cols), F32)
        wt, wt_b = cx.sb(name, (128, nch, cols), BF16)
        for i in range(nch):
            em.dma("sync", lambda e, i=i: e.dma_start(
                out=stg[:], in_=dr[128 * i:128 * (i + 1), :]), writes=[stg_b])
            em.op("vector", lambda e, i=i: e.tensor_copy(out=wt[:, i, :], in_=stg[:]),
                  reads=[stg_b], writes=[wt_b])
        return wt, wt_b

    swq, swq_b = load_weight("swq", wq_in, D, KD3)
    swk, swk_b = load_weight("swk", wk_in, D, KD3)
    swv, swv_b = load_weight("swv", wv_in, D, VD3)
    swg, swg_b = load_weight("swg", wg_in, D, VD3)
    swba, swba_b = load_weight("swba", wba_in, D, 8)

    def load_conv(name, dr, rows):
        nch = rows // 128
        t, b = cx.sb(name, (128, nch, 4), F32)
        for i in range(nch):
            em.dma("sync", lambda e, i=i: e.dma_start(
                out=t[:, i, :], in_=dr[128 * i:128 * (i + 1), :]), writes=[b])
        return t, b

    scq, scq_b = load_conv("scq", cq_in, KD3)
    sck, sck_b = load_conv("sck", ck_in, KD3)
    scv, scv_b = load_conv("scv", cv_in, VD3)

    aldt, aldt_b = cx.sb("s_aldt", (1, 8), F32)
    em.dma("sync", lambda e: e.dma_start(out=aldt[:], in_=aldt_in), writes=[aldt_b])
    onesrow_f, onesrow_f_b = cx.sb("onesrow_f", (1, 128), F32)
    em.op("vector", lambda e: e.memset(onesrow_f[:], 1.0), writes=[onesrow_f_b])
    nalog, nalog_b = cx.sb("nalog", (1, 3), F32)
    em.op("scalar", lambda e: e.activation(out=nalog[:], in_=aldt[0:1, 0:3], func=AF.Exp),
          reads=[aldt_b], writes=[nalog_b])
    em.op("vector", lambda e: e.tensor_scalar(out=nalog[:], in0=nalog[:], scalar1=-1.0,
                                              scalar2=None, op0=ALU.mult),
          reads=[nalog_b], writes=[nalog_b])
    dtb_bf, dtb_bf_b = cx.sb("dtb_bf", (1, 8), BF16)
    em.op("vector", lambda e: e.memset(dtb_bf[:], 0.0), writes=[dtb_bf_b])
    em.op("vector", lambda e: e.tensor_copy(out=dtb_bf[0:1, 3:6], in_=aldt[0:1, 3:6]),
          reads=[aldt_b], writes=[dtb_bf_b])

    pp = [cx.ps(f"pp{i}", (128, 512), F32) for i in range(2)]
    psU, psU_b = cx.ps("psU", (128, 512), F32)
    psV, psV_b = cx.ps("psV", (128, 512), F32)
    psA, psA_b = cx.ps("psA", (128, 128), F32)
    psB, psB_b = cx.ps("psB", (128, 128), F32)
    psT, psT_b = cx.ps("psT", (128, 128), BF16)
    psF, psF_b = cx.ps("psF", (128, 136), F32)

    nalog_bc, nalog_bc_b = cx.sb("nalog_bc", (128, 3), F32)
    mm(psF[:, 128:131], onesrow_f[:], nalog[:], [onesrow_f_b, nalog_b], psF_b)
    em.op("vector", lambda e: e.tensor_copy(out=nalog_bc[:], in_=psF[:, 128:131]),
          reads=[psF_b], writes=[nalog_bc_b])

    xst = [cx.sb(f"xst{i}", (128, 4, D), F32) for i in range(2)]
    xbt = [cx.sb(f"xbt{i}", (128, D), BF16) for i in range(2)]
    xfm = [cx.sb(f"xfm{i}", (128, 8, TT), BF16) for i in range(2)]
    pstg, pstg_b = cx.sb("pstg", (128, 12, TT + 3), BF16)
    qkv = [cx.sb(f"qkv{i}", (128, 12, TT), BF16) for i in range(2)]
    gat = [cx.sb(f"gat{i}", (128, 4, VD3), BF16) for i in range(2)]
    cacc = [cx.sb(f"cacc{i}", (128, TT), F32) for i in range(2)]

    nring = 4
    bcol = [cx.sb(f"bcol{i}", (128, 3), F32) for i in range(nring)]
    becc = [cx.sb(f"becc{i}", (128, 3), F32) for i in range(nring)]
    ecum = [cx.sb(f"ecum{i}", (128, 3), F32) for i in range(nring)]
    cumc = [cx.sb(f"cumc{i}", (128, 3), F32) for i in range(nring)]
    cumt = [cx.sb(f"cumt{i}", (4, 128), F32) for i in range(nring)]
    eccb = [cx.sb(f"eccb{i}", (128, 3), F32) for i in range(nring)]
    efac = [cx.sb(f"efac{i}", (128, 3), F32) for i in range(nring)]
    u1r = [[cx.sb(f"u1_{i}_{h}", (128, DV), BF16) for h in range(3)] for i in range(nring)]
    w1t = [[cx.sb(f"w1t_{i}_{h}", (128, C), BF16) for h in range(3)] for i in range(nring)]
    mtr = [[cx.sb(f"mt_{i}_{h}", (128, C), BF16) for h in range(3)] for i in range(nring)]
    khr = [[cx.sb(f"kh_{i}_{h}", (128, DK), BF16) for h in range(3)] for i in range(nring)]
    sqr = [[cx.sb(f"sq_{i}_{h}", (128, 2), F32) for h in range(3)] for i in range(nring)]

    Sf = [[cx.sb(f"Sf{h}_{p}", (128, DV), F32) for p in range(2)] for h in range(3)]
    Sb = [cx.sb(f"Sb{h}", (128, DV), BF16) for h in range(3)]
    for h in range(3):
        em.op("vector", lambda e, h=h: e.memset(Sf[h][0][0][:], 0.0), writes=[Sf[h][0][1]])
        em.op("vector", lambda e, h=h: e.memset(Sb[h][0][:], 0.0), writes=[Sb[h][1]])

    mcol, mcol_b = cx.sb("mcol", (128, 8), F32)
    lnscr, lnscr_b = cx.sb("lnscr", (128, D), F32)
    rsq, rsq_b = cx.sb("rsq", (128, 4), F32)
    rowb, rowb_b = cx.sb("rowb", (1, 128), F32)
    sqscr, sqscr_b = cx.sb("sqscr", (128, 128), BF16)
    dmat, dmat_b = cx.sb("dmat", (128, C), F32)
    dmat2, dmat2_b = cx.sb("dmat2", (128, C), F32)
    cumbc, cumbc_b = cx.sb("cumbc", (128, C), F32)
    zt = [cx.sb(f"zt{h}", (128, C), BF16) for h in range(3)]
    pw = [cx.sb(f"pw{h}", (128, C), BF16) for h in range(3)]
    pwt = [cx.sb(f"pwt{h}", (128, C), BF16) for h in range(3)]
    amat = [cx.sb(f"amat{h}", (128, C), BF16) for h in range(3)]
    bmask, bmask_b = cx.sb("bmask", (128, C), BF16)
    ktm, ktm_b = cx.sb("ktm", (128, DK), BF16)
    kbtm = [cx.sb(f"kbtm{h}", (128, DK), BF16) for h in range(3)]
    vtm = [cx.sb(f"vtm{h}", (128, DV), BF16) for h in range(3)]
    oscr, oscr_b = cx.sb("oscr", (128, DV), F32)
    oscr2, oscr2_b = cx.sb("oscr2", (128, DV), F32)
    ogb, ogb_b = cx.sb("ogb", (128, DV), BF16)
    ogt, ogt_b = cx.sb("ogt", (128, 2, 128), F32)

    em.op("vector", lambda e: e.memset(pstg[:], 0.0), writes=[pstg_b])

    def phase_a(c, slot, j, qk, qk_b, gt_t, gt_b):
        bc, bc_b = bcol[slot]
        bec, bec_b = becc[slot]
        ec, ec_b = ecum[slot]
        cm, cm_b = cumc[slot]
        ct, ct_b = cumt[slot]
        eb, eb_b = eccb[slot]
        ef, ef_b = efac[slot]
        tc = slice(128 * j, 128 * (j + 1))
        for h in range(3):
            q_ap = qk[:, h, tc]
            k_ap = qk[:, 3 + h, tc]
            v0_ap = qk[:, 6 + 2 * h, tc]
            v1_ap = qk[:, 7 + 2 * h, tc]
            kbtm_t, kbtm_b = kbtm[h]
            vtm_t, vtm_b = vtm[h]
            zt_t, zt_b = zt[h]
            pw_t, pw_b = pw[h]
            pwt_t, pwt_b = pwt[h]
            am_t, am_b = amat[h]
            u1_t, u1_b = u1r[slot][h]
            w1_t, w1_b = w1t[slot][h]
            mt_t, mt_b = mtr[slot][h]
            kh_t, kh_b = khr[slot][h]
            sq_t, sq_b = sqr[slot][h]

            tr(psT[:], k_ap, [qk_b], psT_b)
            em.op("scalar", lambda e: e.activation(out=ktm[:], in_=psT[:], func=AF.Identity),
                  reads=[psT_b], writes=[ktm_b])
            em.op("scalar", lambda e: e.activation(out=sqscr[:], in_=psT[:], func=AF.Square),
                  reads=[psT_b], writes=[sqscr_b])
            em.op("vector", lambda e: e.tensor_reduce(
                out=rsq[:, 0:1], in_=sqscr[:], axis=AX.X, op=ALU.add),
                reads=[sqscr_b], writes=[rsq_b])
            em.op("scalar", lambda e: e.activation(
                out=rsq[:, 1:2], in_=rsq[:, 0:1], func=AF.Ln, bias=cb[:, 1:2]),
                reads=[rsq_b, cb_b], writes=[rsq_b])
            em.op("scalar", lambda e: e.activation(
                out=rsq[:, 1:2], in_=rsq[:, 1:2], func=AF.Exp, scale=-0.5),
                reads=[rsq_b], writes=[rsq_b])
            em.op("scalar", lambda e: e.activation(
                out=ktm[:], in_=ktm[:], func=AF.Identity, scale=rsq[:, 1:2]),
                reads=[ktm_b, rsq_b], writes=[ktm_b])
            tr(psT[:], ktm[:], [ktm_b], psT_b)
            em.op("vector", lambda e, k_ap=k_ap: e.tensor_copy(out=k_ap, in_=psT[:]),
                  reads=[psT_b], writes=[qk_b])
            em.op("scalar", lambda e, kbtm_t=kbtm_t, h=h: e.activation(
                out=kbtm_t[:], in_=ktm[:], func=AF.Identity, scale=bec[:, h:h + 1]),
                reads=[ktm_b, bec_b], writes=[kbtm_b])
            em.op("scalar", lambda e, kh_t=kh_t, h=h: e.activation(
                out=kh_t[:], in_=ktm[:], func=AF.Identity, scale=ef[:, h:h + 1]),
                reads=[ktm_b, ef_b], writes=[kh_b])
            tr(psT[:], v0_ap, [qk_b], psT_b)
            em.op("scalar", lambda e, vtm_t=vtm_t, h=h: e.activation(
                out=vtm_t[:, 0:128], in_=psT[:], func=AF.Identity, scale=bc[:, h:h + 1]),
                reads=[psT_b, bc_b], writes=[vtm_b])
            tr(psT[:], v1_ap, [qk_b], psT_b)
            em.op("scalar", lambda e, vtm_t=vtm_t, h=h: e.activation(
                out=vtm_t[:, 128:256], in_=psT[:], func=AF.Identity, scale=bc[:, h:h + 1]),
                reads=[psT_b, bc_b], writes=[vtm_b])
            em.op("scalar", lambda e, q_ap=q_ap: e.activation(
                out=sqscr[:], in_=q_ap, func=AF.Square), reads=[qk_b], writes=[sqscr_b])
            mm(psU[0:1, 0:128], onescol[:], sqscr[:], [onescol_b, sqscr_b], psU_b)
            em.op("vector", lambda e: e.tensor_copy(out=rowb[:], in_=psU[0:1, 0:128]),
                  reads=[psU_b], writes=[rowb_b])
            tr(psF[:, 0:1], rowb[:], [rowb_b], psF_b, idt=identf, idb=identf_b, p=1)
            em.op("scalar", lambda e: e.activation(
                out=rsq[:, 2:3], in_=psF[:, 0:1], func=AF.Ln, bias=cb[:, 1:2]),
                reads=[psF_b, cb_b], writes=[rsq_b])
            em.op("scalar", lambda e: e.activation(
                out=rsq[:, 2:3], in_=rsq[:, 2:3], func=AF.Exp, scale=-0.5,
                bias=cb[:, 2:3]),
                reads=[rsq_b, cb_b], writes=[rsq_b])
            em.op("vector", lambda e, sq_t=sq_t: e.tensor_copy(
                out=sq_t[:, 0:1], in_=rsq[:, 2:3]), reads=[rsq_b], writes=[sq_b])
            em.op("vector", lambda e, sq_t=sq_t, h=h: e.tensor_tensor(
                out=sq_t[:, 1:2], in0=rsq[:, 2:3], in1=ec[:, h:h + 1], op=ALU.mult),
                reads=[rsq_b, ec_b], writes=[sq_b])
            mm(psU[:, 0:128], sel3[0:3, 128 * h:128 * (h + 1)], ct[0:3, :],
               [ct_b, sel3_b], psU_b)
            em.op("vector", lambda e: e.tensor_copy(out=cumbc[:], in_=psU[:, 0:128]),
                  reads=[psU_b], writes=[cumbc_b])
            em.op("vector", lambda e, h=h: e.tensor_scalar(
                out=dmat[:], in0=cumbc[:], scalar1=-1.0, scalar2=cm[:, h:h + 1],
                op0=ALU.mult, op1=ALU.add),
                reads=[cumbc_b, cm_b], writes=[dmat_b])
            em.op("vector", lambda e: e.tensor_scalar(
                out=dmat[:], in0=dmat[:], scalar1=0.0, scalar2=None, op0=ALU.min),
                reads=[dmat_b], writes=[dmat_b])
            em.op("scalar", lambda e: e.activation(out=dmat[:], in_=dmat[:], func=AF.Exp),
                  reads=[dmat_b], writes=[dmat_b])
            em.op("vector", lambda e, h=h: e.tensor_scalar(
                out=dmat2[:], in0=cumbc[:], scalar1=cm[:, h:h + 1], scalar2=0.0,
                op0=ALU.subtract, op1=ALU.min),
                reads=[cumbc_b, cm_b], writes=[dmat2_b])
            em.op("scalar", lambda e: e.activation(out=dmat2[:], in_=dmat2[:], func=AF.Exp),
                  reads=[dmat2_b], writes=[dmat2_b])
            mm(psA[:], k_ap, k_ap, [qk_b], psA_b)
            em.op("scalar", lambda e, h=h: e.activation(
                out=bmask[:], in_=trils[:], func=AF.Identity, scale=bc[:, h:h + 1]),
                reads=[trils_b, bc_b], writes=[bmask_b])
            em.op("vector", lambda e: e.tensor_tensor(
                out=dmat[:], in0=psA[:], in1=dmat[:], op=ALU.mult),
                reads=[psA_b, dmat_b], writes=[dmat_b])
            em.op("vector", lambda e, am_t=am_t: e.tensor_tensor(
                out=am_t[:], in0=dmat[:], in1=bmask[:], op=ALU.mult),
                reads=[dmat_b, bmask_b], writes=[am_b])
            mm(psB[:], k_ap, q_ap, [qk_b], psB_b)
            em.op("vector", lambda e: e.tensor_tensor(
                out=dmat2[:], in0=psB[:], in1=dmat2[:], op=ALU.mult),
                reads=[psB_b, dmat2_b], writes=[dmat2_b])
            em.op("vector", lambda e, mt_t=mt_t: e.tensor_tensor(
                out=mt_t[:], in0=dmat2[:], in1=triubf[:], op=ALU.mult),
                reads=[dmat2_b, triubf_b], writes=[mt_b])
            tr(psT[:], am_t[:], [am_b], psT_b)
            em.op("vector", lambda e, pwt_t=pwt_t: e.tensor_copy(out=pwt_t[:], in_=psT[:]),
                  reads=[psT_b], writes=[pwt_b])
            em.op("vector", lambda e, zt_t=zt_t, pwt_t=pwt_t: e.tensor_tensor(
                out=zt_t[:], in0=ident[:], in1=pwt_t[:], op=ALU.subtract),
                reads=[ident_b, pwt_b], writes=[zt_b])
            em.op("vector", lambda e, pw_t=pw_t, am_t=am_t: e.tensor_copy(
                out=pw_t[:], in_=am_t[:]), reads=[am_b], writes=[pw_b])
            for lev in range(6):
                mm(psA[:], pwt_t[:], pw_t[:], [pwt_b, pw_b], psA_b)
                if lev < 5:
                    mm(psB[:], pw_t[:], pwt_t[:], [pw_b, pwt_b], psB_b)
                em.op("vector", lambda e, pw_t=pw_t: e.tensor_copy(out=pw_t[:], in_=psA[:]),
                      reads=[psA_b], writes=[pw_b])
                if lev < 5:
                    em.op("vector", lambda e, pwt_t=pwt_t: e.tensor_copy(
                        out=pwt_t[:], in_=psB[:]), reads=[psB_b], writes=[pwt_b])
                mm(psA[:], pw_t[:], zt_t[:], [pw_b, zt_b], psA_b)
                em.op("vector", lambda e, zt_t=zt_t: e.tensor_tensor(
                    out=zt_t[:], in0=psA[:], in1=zt_t[:], op=ALU.add),
                    reads=[psA_b, zt_b], writes=[zt_b])
            mm(psU[:, 0:DV], zt_t[:], vtm_t[:], [zt_b, vtm_b], psU_b)
            em.op("vector", lambda e, u1_t=u1_t: e.tensor_copy(out=u1_t[:], in_=psU[:, 0:DV]),
                  reads=[psU_b], writes=[u1_b])
            mm(psB[:], kbtm_t[:], zt_t[:], [kbtm_b, zt_b], psB_b)
            em.op("vector", lambda e, w1_t=w1_t: e.tensor_copy(out=w1_t[:], in_=psB[:]),
                  reads=[psB_b], writes=[w1_b])

    def phase_b(c, slot, j, qk, qk_b, gt_t, gt_b):
        eb, eb_b = eccb[slot]
        tc = slice(128 * j, 128 * (j + 1))
        for h in range(3):
            q_ap = qk[:, h, tc]
            u1_t, u1_b = u1r[slot][h]
            w1_t, w1_b = w1t[slot][h]
            mt_t, mt_b = mtr[slot][h]
            kh_t, kh_b = khr[slot][h]
            sq_t, sq_b = sqr[slot][h]
            sf_t, sf_b = Sf[h][c % 2]
            sf2_t, sf2_b = Sf[h][(c + 1) % 2]
            sb_t, sb_b = Sb[h]
            mm(psU[:, 0:DV], w1_t[:], sb_t[:], [w1_b, sb_b], psU_b)
            em.op("vector", lambda e, u1_t=u1_t: e.tensor_tensor(
                out=u1_t[:], in0=u1_t[:], in1=psU[:, 0:DV], op=ALU.subtract),
                reads=[u1_b, psU_b], writes=[u1_b])
            mm(psV[:, 0:DV], q_ap, sb_t[:], [qk_b, sb_b], psV_b)
            em.op("scalar", lambda e, sq_t=sq_t: e.activation(
                out=oscr[:], in_=psV[:, 0:DV], func=AF.Identity, scale=sq_t[:, 1:2]),
                reads=[psV_b, sq_b], writes=[oscr_b])
            mm(psU[:, 0:DV], mt_t[:], u1_t[:], [mt_b, u1_b], psU_b)
            em.op("vector", lambda e, sq_t=sq_t: e.scalar_tensor_tensor(
                out=oscr[:], in0=psU[:, 0:DV], scalar=sq_t[:, 0:1], in1=oscr[:],
                op0=ALU.mult, op1=ALU.add),
                reads=[psU_b, sq_b, oscr_b], writes=[oscr_b])
            em.op("vector", lambda e: e.tensor_tensor(
                out=oscr2[:], in0=oscr[:], in1=oscr[:], op=ALU.mult),
                reads=[oscr_b], writes=[oscr2_b])
            em.op("vector", lambda e: e.tensor_reduce(
                out=rsq[:, 0:1], in_=oscr2[:], axis=AX.X, op=ALU.add),
                reads=[oscr2_b], writes=[rsq_b])
            em.op("scalar", lambda e: e.activation(
                out=rsq[:, 1:2], in_=rsq[:, 0:1], func=AF.Ln, scale=1.0 / DV,
                bias=cb[:, 0:1]),
                reads=[rsq_b, cb_b], writes=[rsq_b])
            em.op("scalar", lambda e: e.activation(
                out=rsq[:, 1:2], in_=rsq[:, 1:2], func=AF.Exp, scale=-0.5),
                reads=[rsq_b], writes=[rsq_b])
            em.op("vector", lambda e, h=h, gt_t=gt_t, j=j: e.scalar_tensor_tensor(
                out=ogb[:], in0=oscr[:], scalar=rsq[:, 1:2],
                in1=gt_t[:, j, 256 * h:256 * (h + 1)], op0=ALU.mult, op1=ALU.mult),
                reads=[oscr_b, rsq_b, gt_b], writes=[ogb_b])
            mm(psV[:, 0:DV], kh_t[:], u1_t[:], [kh_b, u1_b], psV_b)
            em.op("vector", lambda e, sf_t=sf_t, sf2_t=sf2_t, h=h: e.scalar_tensor_tensor(
                out=sf2_t[:], in0=sf_t[:], scalar=eb[:, h:h + 1], in1=psV[:, 0:DV],
                op0=ALU.mult, op1=ALU.add),
                reads=[sf_b, eb_b, psV_b], writes=[sf2_b])
            em.op("scalar", lambda e, sf2_t=sf2_t, sb_t=sb_t: e.activation(
                out=sb_t[:], in_=sf2_t[:], func=AF.Identity),
                reads=[sf2_b], writes=[sb_b])
            for half in range(2):
                tr(psT[:], ogb[:, 128 * half:128 * (half + 1)], [ogb_b], psT_b)
                em.op("vector", lambda e, half=half: e.tensor_copy(
                    out=ogt[:, half, :], in_=psT[:]), reads=[psT_b], writes=[ogt_b])
            for half in range(2):
                r0 = 256 * h + 128 * half
                em.dma("scalar", lambda e, half=half, r0=r0, c=c: e.dma_start(
                    out=og_out[r0:r0 + 128, 128 * c:128 * (c + 1)], in_=ogt[:, half, :]),
                    reads=[ogt_b], writes=[])

    def do_mt(mt, xs, xs_b, xb, xb_b, xf, xf_b, qk, qk_b, gt_t, gt_b):
        for jj in range(4):
            r0 = mt * TT + jj * 128
            em.dma("sync", lambda e, jj=jj, r0=r0: e.dma_start(
                out=xs[:, jj, :], in_=x_in[r0:r0 + 128, :]), writes=[xs_b])
        for jj in range(4):
            em.op("vector", lambda e, jj=jj: e.tensor_reduce(
                out=mcol[:, 0:1], in_=xs[:, jj, :], axis=AX.X, op=ALU.add),
                reads=[xs_b], writes=[mcol_b])
            em.op("scalar", lambda e, jj=jj: e.activation(
                out=lnscr[:], in_=xs[:, jj, :], func=AF.Square),
                reads=[xs_b], writes=[lnscr_b])
            em.op("vector", lambda e: e.tensor_reduce(
                out=mcol[:, 1:2], in_=lnscr[:], axis=AX.X, op=ALU.add),
                reads=[lnscr_b], writes=[mcol_b])
            em.op("vector", lambda e: e.tensor_scalar(
                out=mcol[:, 2:3], in0=mcol[:, 0:1], scalar1=1.0 / D, scalar2=None,
                op0=ALU.mult), reads=[mcol_b], writes=[mcol_b])
            em.op("vector", lambda e: e.tensor_tensor(
                out=mcol[:, 3:4], in0=mcol[:, 2:3], in1=mcol[:, 2:3], op=ALU.mult),
                reads=[mcol_b], writes=[mcol_b])
            em.op("vector", lambda e: e.tensor_scalar(
                out=mcol[:, 4:5], in0=mcol[:, 1:2], scalar1=1.0 / D, scalar2=mcol[:, 3:4],
                op0=ALU.mult, op1=ALU.subtract), reads=[mcol_b], writes=[mcol_b])
            em.op("scalar", lambda e: e.activation(
                out=mcol[:, 5:6], in_=mcol[:, 4:5], func=AF.Ln, bias=cb[:, 0:1]),
                reads=[mcol_b, cb_b], writes=[mcol_b])
            em.op("scalar", lambda e: e.activation(
                out=mcol[:, 6:7], in_=mcol[:, 5:6], func=AF.Exp, scale=-0.5),
                reads=[mcol_b], writes=[mcol_b])
            em.op("vector", lambda e, jj=jj: e.tensor_scalar(
                out=xb[:], in0=xs[:, jj, :], scalar1=mcol[:, 2:3], scalar2=mcol[:, 6:7],
                op0=ALU.subtract, op1=ALU.mult),
                reads=[xs_b, mcol_b], writes=[xb_b])
            for ic in range(8):
                tr(psT[:], xb[:, 128 * ic:128 * (ic + 1)], [xb_b], psT_b)
                em.op("vector", lambda e, jj=jj, ic=ic: e.tensor_copy(
                    out=xf[:, ic, 128 * jj:128 * (jj + 1)], in_=psT[:]),
                    reads=[psT_b], writes=[xf_b])
        em.op("vector", lambda e: e.tensor_copy(out=pstg[:, :, 0:3], in_=pstg[:, :, TT:TT + 3]),
              reads=[pstg_b], writes=[pstg_b])
        for oc in range(12):
            if oc < 3:
                wsrc, wsrc_b, off = swq, swq_b, oc
            elif oc < 6:
                wsrc, wsrc_b, off = swk, swk_b, oc - 3
            else:
                wsrc, wsrc_b, off = swv, swv_b, oc - 6
            ps, ps_b = pp[oc % 2]
            for ic in range(8):
                mm(ps[:, 0:TT], wsrc[:, ic, 128 * off:128 * (off + 1)], xf[:, ic, :],
                   [wsrc_b, xf_b], ps_b, start=(ic == 0), stop=(ic == 7))
            em.op("scalar", lambda e, oc=oc, ps=ps: e.activation(
                out=pstg[:, oc, 3:TT + 3], in_=ps[:, 0:TT], func=AF.Identity),
                reads=[ps_b], writes=[pstg_b])
        for oc in range(12):
            if oc < 3:
                cw, cw_b, ci = scq, scq_b, oc
            elif oc < 6:
                cw, cw_b, ci = sck, sck_b, oc - 3
            else:
                cw, cw_b, ci = scv, scv_b, oc - 6
            ca, ca_b = cacc[oc % 2]
            em.op("vector", lambda e, oc=oc, ci=ci, ca=ca, cw=cw: e.tensor_scalar(
                out=ca[:], in0=pstg[:, oc, 3:TT + 3], scalar1=cw[:, ci, 3:4], scalar2=None,
                op0=ALU.mult), reads=[pstg_b, cw_b], writes=[ca_b])
            for tap in range(3):
                eng = "vector"
                em.op(eng, lambda e, oc=oc, ci=ci, tap=tap, ca=ca, cw=cw: e.scalar_tensor_tensor(
                    out=ca[:], in0=pstg[:, oc, tap:TT + tap], scalar=cw[:, ci, tap:tap + 1],
                    in1=ca[:], op0=ALU.mult, op1=ALU.add),
                    reads=[pstg_b, cw_b, ca_b], writes=[ca_b])
            em.op("scalar", lambda e, oc=oc, ca=ca: e.activation(
                out=qk[:, oc, :], in_=ca[:], func=AF.Silu), reads=[ca_b], writes=[qk_b])
        for jj in range(4):
            ps, ps_b = pp[0]
            ps2, ps2_b = pp[1]
            for ic in range(8):
                mm(ps[:, 0:512], xf[:, ic, 128 * jj:128 * (jj + 1)], swg[:, ic, 0:512],
                   [xf_b, swg_b], ps_b, start=(ic == 0), stop=(ic == 7))
            for ic in range(8):
                mm(ps2[:, 0:256], xf[:, ic, 128 * jj:128 * (jj + 1)], swg[:, ic, 512:768],
                   [xf_b, swg_b], ps2_b, start=(ic == 0), stop=(ic == 7))
            em.op("scalar", lambda e, jj=jj, ps=ps: e.activation(
                out=gt_t[:, jj, 0:512], in_=ps[:, 0:512], func=AF.Silu),
                reads=[ps_b], writes=[gt_b])
            em.op("scalar", lambda e, jj=jj, ps2=ps2: e.activation(
                out=gt_t[:, jj, 512:768], in_=ps2[:, 0:256], func=AF.Silu),
                reads=[ps2_b], writes=[gt_b])
        for jj in range(4):
            c = mt * 4 + jj
            slot = c % nring
            bc, bc_b = bcol[slot]
            bec, bec_b = becc[slot]
            ec, ec_b = ecum[slot]
            cm, cm_b = cumc[slot]
            ct, ct_b = cumt[slot]
            eb, eb_b = eccb[slot]
            ef, ef_b = efac[slot]
            for ic in range(8):
                mm(psF[:, 128:136], xf[:, ic, 128 * jj:128 * (jj + 1)], swba[:, ic, 0:8],
                   [xf_b, swba_b], psF_b, start=(ic == 0), stop=(ic == 7))
            mm(psF[:, 128:136], onesrow[:], dtb_bf[:], [onesrow_b, dtb_bf_b],
               psF_b, start=False, stop=True)
            if getattr(cx, "debug", False) and c == 0:
                dpre, dpre_b = cx.sb("dpre", (128, 12), F32)
                em.op("vector", lambda e: e.tensor_copy(out=dpre[:, 0:8], in_=psF[:, 128:136]),
                      reads=[psF_b], writes=[dpre_b])
                cx.dpre = (dpre, dpre_b)
            em.op("scalar", lambda e: e.activation(
                out=mcol[:, 0:3], in_=psF[:, 128:131], func=AF.Exp, scale=-1.0),
                reads=[psF_b], writes=[mcol_b])
            if getattr(cx, "debug", False) and c == 0:
                dpre, dpre_b = cx.dpre
                em.op("vector", lambda e: e.tensor_copy(out=dpre[:, 8:11], in_=mcol[:, 0:3]),
                      reads=[mcol_b], writes=[dpre_b])
            em.op("vector", lambda e: e.tensor_scalar(
                out=mcol[:, 0:3], in0=mcol[:, 0:3], scalar1=1.0, scalar2=None, op0=ALU.add),
                reads=[mcol_b], writes=[mcol_b])
            em.op("vector", lambda e, bc=bc: e.reciprocal(out=bc[:], in_=mcol[:, 0:3]),
                  reads=[mcol_b], writes=[bc_b])
            em.op("scalar", lambda e: e.activation(
                out=mcol[:, 0:3], in_=psF[:, 131:134], func=AF.Abs),
                reads=[psF_b], writes=[mcol_b])
            em.op("scalar", lambda e: e.activation(
                out=mcol[:, 0:3], in_=mcol[:, 0:3], func=AF.Exp, scale=-1.0),
                reads=[mcol_b], writes=[mcol_b])
            em.op("scalar", lambda e: e.activation(
                out=mcol[:, 0:3], in_=mcol[:, 0:3], func=AF.Ln, bias=cb[:, 3:4]),
                reads=[mcol_b, cb_b], writes=[mcol_b])
            em.op("vector", lambda e: e.tensor_scalar(
                out=mcol[:, 3:6], in0=psF[:, 131:134], scalar1=0.0, scalar2=None,
                op0=ALU.max), reads=[psF_b], writes=[mcol_b])
            em.op("vector", lambda e: e.tensor_tensor(
                out=mcol[:, 0:3], in0=mcol[:, 0:3], in1=mcol[:, 3:6], op=ALU.add),
                reads=[mcol_b], writes=[mcol_b])
            em.op("vector", lambda e: e.tensor_tensor(
                out=mcol[:, 0:3], in0=mcol[:, 0:3], in1=nalog_bc[:], op=ALU.mult),
                reads=[mcol_b, nalog_bc_b], writes=[mcol_b])
            mm(psF[:, 128:131], triu[:], mcol[:, 0:3], [triu_b, mcol_b], psF_b)
            em.op("vector", lambda e, cm=cm: e.tensor_copy(out=cm[:], in_=psF[:, 128:131]),
                  reads=[psF_b], writes=[cm_b])
            em.op("scalar", lambda e, ec=ec, cm=cm: e.activation(
                out=ec[:], in_=cm[:], func=AF.Exp), reads=[cm_b], writes=[ec_b])
            tr(psF[0:3, 0:128], cm[:, 0:3], [cm_b], psF_b, idt=identf, idb=identf_b)
            em.op("vector", lambda e, ct=ct: e.tensor_copy(out=ct[0:3, :], in_=psF[0:3, 0:128]),
                  reads=[psF_b], writes=[ct_b])
            mm(psF[:, 128:131], sel_last[:], cm[:], [sel_last_b, cm_b], psF_b)
            em.op("scalar", lambda e, eb=eb: e.activation(
                out=eb[:], in_=psF[:, 128:131], func=AF.Exp), reads=[psF_b], writes=[eb_b])
            em.op("vector", lambda e, cm=cm: e.tensor_tensor(
                out=mcol[:, 3:6], in0=psF[:, 128:131], in1=cm[:], op=ALU.subtract),
                reads=[psF_b, cm_b], writes=[mcol_b])
            em.op("scalar", lambda e, ef=ef: e.activation(
                out=ef[:], in_=mcol[:, 3:6], func=AF.Exp), reads=[mcol_b], writes=[ef_b])
            em.op("vector", lambda e, bec=bec, bc=bc, ec=ec: e.tensor_tensor(
                out=bec[:], in0=bc[:], in1=ec[:], op=ALU.mult),
                reads=[bc_b, ec_b], writes=[bec_b])
            phase_a(c, slot, jj, qk, qk_b, gt_t, gt_b)
            if getattr(cx, "debug", False) and c == 0:
                dumps = [
                    ("xf0", xf[:, 0, 0:128], xf_b, BF16, 128),
                    ("q0", qk[:, 0, 0:128], qk_b, BF16, 128),
                    ("k0n", qk[:, 3, 0:128], qk_b, BF16, 128),
                    ("v0", qk[:, 6, 0:128], qk_b, BF16, 128),
                    ("gate0", gt_t[:, 0, 0:256], gt_b, BF16, 256),
                    ("bc", bc[:], bc_b, F32, 3),
                    ("cum", cm[:], cm_b, F32, 3),
                    ("ecum", ec[:], ec_b, F32, 3),
                    ("eb", eb[:], eb_b, F32, 3),
                    ("ef", ef[:], ef_b, F32, 3),
                    ("am0", amat[0][0][:], amat[0][1], BF16, 128),
                    ("zt0", zt[0][0][:], zt[0][1], BF16, 128),
                    ("u10", u1r[slot][0][0][:], u1r[slot][0][1], BF16, 256),
                    ("w1t0", w1t[slot][0][0][:], w1t[slot][0][1], BF16, 128),
                    ("mt0", mtr[slot][0][0][:], mtr[slot][0][1], BF16, 128),
                    ("kh0", khr[slot][0][0][:], khr[slot][0][1], BF16, 128),
                    ("sq0", sqr[slot][0][0][:], sqr[slot][0][1], F32, 2),
                    ("dpre", cx.dpre[0][:], cx.dpre[1], F32, 12),
                    ("ktm0", ktm[:], ktm_b, BF16, 128),
                    ("vtm0", vtm[0][0][:], vtm[0][1], BF16, 256),
                    ("kbtm0", kbtm[0][0][:], kbtm[0][1], BF16, 128),
                ]
                dstg, dstg_b = cx.sb("dstg", (128, 256), F32)
                colofs = 0
                cx.dbg_layout = []
                for name, ap, rbuf, dt, w in dumps:
                    em.op("vector", lambda e, ap=ap, w=w: e.tensor_copy(
                        out=dstg[:, 0:w], in_=ap), reads=[rbuf], writes=[dstg_b])
                    em.dma("scalar", lambda e, w=w, co=colofs: e.dma_start(
                        out=cx.dbg_out[0:128, co:co + w], in_=dstg[:, 0:w]),
                        reads=[dstg_b], writes=[])
                    cx.dbg_layout.append((name, colofs, w))
                    colofs += w
            phase_b(c, slot, jj, qk, qk_b, gt_t, gt_b)
            if getattr(cx, "debug", False) and c == 0:
                dumps2 = [
                    ("S1", Sb[0][0][:], Sb[0][1], BF16, 256),
                    ("u0fin", u1r[slot][0][0][:], u1r[slot][0][1], BF16, 256),
                ]
                for name, ap, rbuf, dt, w in dumps2:
                    em.op("vector", lambda e, ap=ap, w=w: e.tensor_copy(
                        out=dstg[:, 0:w], in_=ap), reads=[rbuf], writes=[dstg_b])
                    em.dma("scalar", lambda e, w=w, co=colofs: e.dma_start(
                        out=cx.dbg_out[0:128, co:co + w], in_=dstg[:, 0:w]),
                        reads=[dstg_b], writes=[])
                    cx.dbg_layout.append((name, colofs, w))
                    colofs += w

    for mt in range(NT):
        do_mt(mt, *xst[mt % 2], *xbt[mt % 2], *xfm[mt % 2], *qkv[mt % 2], *gat[mt % 2])



TP2 = 2048


def build_ffn_program(nt2=None, debug=False):
    nc = bass.Bass("TRN2", target_bir_lowering=False)
    em = Emitter()
    with ExitStack() as st:
        cx = Ctx(nc, em, st)
        x_in = nc.dram_tensor("x", [TP2, D], F32, kind="ExternalInput").ap()
        og_in = nc.dram_tensor("og", [H * DV, TP2], F32, kind="ExternalInput").ap()
        wo_in = nc.dram_tensor("wo", [H * DV, D], F32, kind="ExternalInput").ap()
        nw_in = nc.dram_tensor("nw", [H * DV, 1], F32, kind="ExternalInput").ap()
        wfc_in = nc.dram_tensor("wfc", [FH, D], F32, kind="ExternalInput").ap()
        wpr_in = nc.dram_tensor("wpr", [D, FH], F32, kind="ExternalInput").ap()
        u_in = nc.dram_tensor("lru", [FH, RK], F32, kind="ExternalInput").ap()
        v_in = nc.dram_tensor("lrv", [RK, D], F32, kind="ExternalInput").ap()
        fb_in = nc.dram_tensor("lrfb", [RK, RK], F32, kind="ExternalInput").ap()
        gate_in = nc.dram_tensor("lrgate", [1, 1], F32, kind="ExternalInput").ap()
        y_out = nc.dram_tensor("y", [TP2, D], F32, kind="ExternalOutput").ap()
        if debug:
            cx.dbgy = nc.dram_tensor("dbgy", [TP2, D], F32, kind="ExternalOutput").ap()
            cx.dbgr2 = nc.dram_tensor("dbgr2", [FH, TT], F32, kind="ExternalOutput").ap()
        wfc_sc = nc.dram_tensor("wfc_scratch", [8, 32, 128, 128], BF16, kind="Internal").ap()
        wpr_sc = nc.dram_tensor("wpr_scratch", [FH, D], BF16, kind="Internal").ap()

        cc = load_constants(cx, em)
        cx.wfc_sc_b = Buf("wfc_sc")
        cx.wpr_sc_b = Buf("wpr_sc")
        build_ffn_body(cx, em, cc, x_in, og_in, wo_in, nw_in, wfc_in, wpr_in,
                       u_in, v_in, fb_in, gate_in, y_out, wfc_sc, wpr_sc, nt2=nt2)
        sems = {}
        with ExitStack() as st2:
            for s in STREAMS:
                for k in ("c", "d"):
                    sems[(s, k)] = st2.enter_context(nc.semaphore(f"sem_{s}_{k}"))
            block = st2.enter_context(nc.Block())
            em.emit(nc, block, sems)
    return nc


def build_ffn_body(cx, em, cc, x_in, og_in, wo_in, nw_in, wfc_in, wpr_in,
                   u_in, v_in, fb_in, gate_in, y_out, wfc_sc, wpr_sc, nt2=None):
    nc = cx.nc
    ident, ident_b = cc["c_ident"]
    identf, identf_b = cc["c_ident_f"]
    cb, cb_b = cc["cbias"]

    def mm(out_ap, lhsT_ap, rhs_ap, reads, wbuf, start=True, stop=True):
        em.op("tensor", lambda e: e.matmul(out=out_ap, lhsT=lhsT_ap, rhs=rhs_ap,
                                           start=start, stop=stop),
              reads=list(reads), writes=[wbuf])

    def tr(out_ap, in_ap, reads, wbuf, p=128):
        em.op("tensor", lambda e: e.transpose(out=out_ap, in_=in_ap, identity=ident[0:p, 0:p]),
              reads=list(reads) + [ident_b], writes=[wbuf])

    pp = [cx.ps(f"fp{i}", (128, 512), F32) for i in range(4)]
    psT, psT_b = cx.ps("psT", (128, 128), BF16)
    psS, psS_b = cx.ps("psS", (128, 128), F32)

    swo, swo_b = cx.sb("swo", (128, 12, D), BF16)
    stg, stg_b = cx.sb("wstg", (128, D), F32)
    nwc, nwc_b = cx.sb("nwc", (128, 12), F32)
    for i in range(12):
        em.dma("sync", lambda e, i=i: e.dma_start(
            out=nwc[:, i:i + 1], in_=nw_in[128 * i:128 * (i + 1), :]), writes=[nwc_b])
    for i in range(12):
        em.dma("sync", lambda e, i=i: e.dma_start(
            out=stg[:], in_=wo_in[128 * i:128 * (i + 1), :]), writes=[stg_b])
        em.op("scalar", lambda e, i=i: e.activation(
            out=swo[:, i, :], in_=stg[:], func=AF.Identity, scale=nwc[:, i:i + 1]),
            reads=[stg_b, nwc_b], writes=[swo_b])

    svt, svt_b = cx.sb("svt", (128, 8, RK), BF16)
    s16, s16_b = cx.sb("s16", (16, 1024), F32)
    sv16, sv16_b = cx.sb("sv16", (16, 1024), BF16)
    em.dma("sync", lambda e: e.dma_start(out=s16[0:16, 0:1024], in_=v_in), writes=[s16_b])
    em.op("vector", lambda e: e.tensor_copy(out=sv16[:], in_=s16[:]),
          reads=[s16_b], writes=[sv16_b])
    for i in range(8):
        tr(psT[0:128, 0:16], sv16[0:16, 128 * i:128 * (i + 1)], [sv16_b], psT_b, p=16)
        em.op("vector", lambda e, i=i: e.tensor_copy(out=svt[:, i, :], in_=psT[0:128, 0:16]),
              reads=[psT_b], writes=[svt_b])
    sfbt, sfbt_b = cx.sb("sfbt", (16, RK), BF16)
    em.dma("sync", lambda e: e.dma_start(out=s16[0:16, 0:16], in_=fb_in), writes=[s16_b])
    em.op("vector", lambda e: e.tensor_copy(out=sv16[0:16, 0:16], in_=s16[0:16, 0:16]),
          reads=[s16_b], writes=[sv16_b])
    tr(psT[0:16, 0:16], sv16[0:16, 0:16], [sv16_b], psT_b, p=16)
    em.op("vector", lambda e: e.tensor_copy(out=sfbt[:], in_=psT[0:16, 0:16]),
          reads=[psT_b], writes=[sfbt_b])
    sgate, sgate_b = cx.sb("sgate", (1, 1), F32)
    em.dma("sync", lambda e: e.dma_start(out=sgate[:], in_=gate_in), writes=[sgate_b])
    sgcol, sgcol_b = cx.sb("sgcol", (16, 1), F32)
    onesrow_f, onesrow_f_b = cx.sb("onesrow_f", (1, 128), F32)
    em.op("vector", lambda e: e.memset(onesrow_f[:], 1.0), writes=[onesrow_f_b])
    mm(psS[0:16, 0:1], onesrow_f[0:1, 0:16], sgate[:], [onesrow_f_b, sgate_b], psS_b)
    em.op("vector", lambda e: e.tensor_copy(out=sgcol[:], in_=psS[0:16, 0:1]),
          reads=[psS_b], writes=[sgcol_b])
    sut, sut_b = cx.sb("sut", (16, 32, 128), BF16)
    sustg, sustg_b = cx.sb("sustg", (128, RK), F32)
    sustb, sustb_b = cx.sb("sustb", (128, RK), BF16)
    for i in range(32):
        em.dma("sync", lambda e, i=i: e.dma_start(
            out=sustg[:], in_=u_in[128 * i:128 * (i + 1), :]), writes=[sustg_b])
        em.op("vector", lambda e: e.tensor_copy(out=sustb[:], in_=sustg[:]),
              reads=[sustg_b], writes=[sustb_b])
        tr(psT[0:16, 0:128], sustb[:], [sustb_b], psT_b)
        em.op("scalar", lambda e, i=i: e.activation(
            out=sut[:, i, :], in_=psT[0:16, 0:128], func=AF.Identity, scale=sgcol[:]),
            reads=[psT_b, sgcol_b], writes=[sut_b])

    tscr, tscr_b = cx.sb("tscr", (128, D), F32)
    tmsk, tmsk_b = cx.sb("tmsk", (128, D), F32)
    tcol, tcol_b = cx.sb("tcol", (128, 2), F32)
    wrow, wrow_b = cx.sb("wrow", (128, D), BF16)
    wblk, wblk_b = cx.sb("wblk", (128, 128), BF16)
    for i in range(32):
        em.dma("sync", lambda e, i=i: e.dma_start(
            out=tscr[:], in_=wfc_in[128 * i:128 * (i + 1), :]), writes=[tscr_b])
        em.op("scalar", lambda e: e.activation(out=tmsk[:], in_=tscr[:], func=AF.Abs),
              reads=[tscr_b], writes=[tmsk_b])
        nred = D
        while nred > 1:
            h_ = nred // 2
            em.op("vector", lambda e, h_=h_, nred=nred: e.tensor_tensor(
                out=tmsk[:, 0:h_], in0=tmsk[:, 0:h_], in1=tmsk[:, h_:nred], op=ALU.add),
                reads=[tmsk_b], writes=[tmsk_b])
            nred = h_
        em.op("vector", lambda e: e.tensor_scalar(
            out=tcol[:, 0:1], in0=tmsk[:, 0:1], scalar1=1.0 / D, scalar2=1e-5,
            op0=ALU.mult, op1=ALU.max), reads=[tmsk_b], writes=[tcol_b])
        em.op("vector", lambda e: e.tensor_scalar(
            out=tmsk[:], in0=tscr[:], scalar1=2.0, scalar2=tcol[:, 0:1],
            op0=ALU.mult, op1=ALU.is_gt), reads=[tscr_b, tcol_b], writes=[tmsk_b])
        em.op("gpsimd", lambda e: e.tensor_scalar(
            out=tscr[:], in0=tscr[:], scalar1=-2.0, scalar2=tcol[:, 0:1],
            op0=ALU.mult, op1=ALU.is_gt), reads=[tscr_b, tcol_b], writes=[tscr_b])
        em.op("vector", lambda e: e.tensor_tensor(
            out=tmsk[:], in0=tmsk[:], in1=tscr[:], op=ALU.subtract),
            reads=[tmsk_b, tscr_b], writes=[tmsk_b])
        em.op("scalar", lambda e: e.activation(
            out=wrow[:], in_=tmsk[:], func=AF.Identity, scale=tcol[:, 0:1]),
            reads=[tmsk_b, tcol_b], writes=[wrow_b])
        for ic in range(8):
            tr(psT[:], wrow[:, 128 * ic:128 * (ic + 1)], [wrow_b], psT_b)
            em.op("vector", lambda e: e.tensor_copy(out=wblk[:], in_=psT[:]),
                  reads=[psT_b], writes=[wblk_b])
            em.dma("scalar", lambda e, i=i, ic=ic: e.dma_start(
                out=wfc_sc[ic, i, :, :], in_=wblk[:]), reads=[wblk_b],
                writes=[cx.wfc_sc_b])

    wps, wps_b = cx.sb("wps", (128, FH), F32)
    awp, awp_b = cx.sb("awp", (128, FH), F32)
    wprow, wprow_b = cx.sb("wprow", (128, FH), BF16)
    for i in range(8):
        em.dma("sync", lambda e, i=i: e.dma_start(
            out=wps[:], in_=wpr_in[128 * i:128 * (i + 1), :]), writes=[wps_b])
        em.op("scalar", lambda e: e.activation(out=awp[:], in_=wps[:], func=AF.Abs),
              reads=[wps_b], writes=[awp_b])
        nred = FH
        while nred > 1:
            h_ = nred // 2
            em.op("vector", lambda e, h_=h_, nred=nred: e.tensor_tensor(
                out=awp[:, 0:h_], in0=awp[:, 0:h_], in1=awp[:, h_:nred], op=ALU.add),
                reads=[awp_b], writes=[awp_b])
            nred = h_
        em.op("vector", lambda e: e.tensor_scalar(
            out=tcol[:, 1:2], in0=awp[:, 0:1], scalar1=1.0 / FH, scalar2=1e-5,
            op0=ALU.mult, op1=ALU.max), reads=[awp_b], writes=[tcol_b])
        for cc in range(4):
            sl = slice(1024 * cc, 1024 * (cc + 1))
            em.op("vector", lambda e, sl=sl: e.tensor_scalar(
                out=tmsk[:], in0=wps[:, sl], scalar1=2.0, scalar2=tcol[:, 1:2],
                op0=ALU.mult, op1=ALU.is_gt), reads=[wps_b, tcol_b], writes=[tmsk_b])
            em.op("vector", lambda e, sl=sl: e.tensor_scalar(
                out=tscr[:], in0=wps[:, sl], scalar1=-2.0, scalar2=tcol[:, 1:2],
                op0=ALU.mult, op1=ALU.is_gt), reads=[wps_b, tcol_b], writes=[tscr_b])
            em.op("vector", lambda e: e.tensor_tensor(
                out=tmsk[:], in0=tmsk[:], in1=tscr[:], op=ALU.subtract),
                reads=[tmsk_b, tscr_b], writes=[tmsk_b])
            em.op("scalar", lambda e, sl=sl: e.activation(
                out=wprow[:, sl], in_=tmsk[:], func=AF.Identity, scale=tcol[:, 1:2]),
                reads=[tmsk_b, tcol_b], writes=[wprow_b])
        for jc in range(32):
            tr(psT[:], wprow[:, 128 * jc:128 * (jc + 1)], [wprow_b], psT_b)
            em.op("vector", lambda e: e.tensor_copy(out=wblk[:], in_=psT[:]),
                  reads=[psT_b], writes=[wblk_b])
            em.dma("scalar", lambda e, i=i, jc=jc: e.dma_start(
                out=wpr_sc[128 * jc:128 * (jc + 1), 128 * i:128 * (i + 1)],
                in_=wblk[:]), reads=[wblk_b], writes=[cx.wpr_sc_b])

    xti = [cx.sb(f"xti{i}", (128, D), F32) for i in range(2)]
    ogi = [cx.sb(f"ogi{i}", (128, 12, 128), BF16) for i in range(2)]
    ogf = [cx.sb(f"ogf{i}", (128, 12, 128), F32) for i in range(1)]
    yti = [cx.sb(f"yti{i}", (128, 4, D), F32) for i in range(1)]
    x2b, x2b_b = cx.sb("x2b", (128, D), BF16)
    x2f = [cx.sb(f"x2f{i}", (128, 8, TT), BF16) for i in range(1)]
    r2t, r2t_b = cx.sb("r2", (128, 32, TT), BF16)
    c1, c1_b = cx.sb("c1", (16, TT), BF16)
    c2, c2_b = cx.sb("c2", (16, TT), BF16)
    wef = [cx.sb(f"wef{i}", (128, 128), BF16) for i in range(2)]
    wpt = [cx.sb(f"wpt{i}", (128, 512), BF16) for i in range(2)]
    mc2, mc2_b = cx.sb("mc2", (128, 8), F32)
    outt, outt_b = cx.sb("outt", (128, D), F32)

    NT2 = TP2 // TT

    def do_t5(t5, yt, yt_b, xf2, xf2_b):
        for jj in range(4):
            tk = t5 * TT + jj * 128
            xt, xt_b = xti[jj % 2]
            og_t, og_b = ogi[jj % 2]
            ogf_t, ogf_b = ogf[0]
            em.dma("sync", lambda e, tk=tk, xt=xt: e.dma_start(
                out=xt[:], in_=x_in[tk:tk + 128, :]), writes=[xt_b])
            for i in range(12):
                em.dma("gpsimd", lambda e, i=i, tk=tk, ogf_t=ogf_t: e.dma_start(
                    out=ogf_t[:, i, :], in_=og_in[128 * i:128 * (i + 1), tk:tk + 128]),
                    writes=[ogf_b])
            for i in range(12):
                em.op("vector", lambda e, i=i, og_t=og_t, ogf_t=ogf_t: e.tensor_copy(
                    out=og_t[:, i, :], in_=ogf_t[:, i, :]), reads=[ogf_b], writes=[og_b])
            ps, ps_b = pp[0]
            ps2, ps2_b = pp[1]
            for i in range(12):
                mm(ps[:, 0:512], og_t[:, i, :], swo[:, i, 0:512],
                   [og_b, swo_b], ps_b, start=(i == 0), stop=(i == 11))
            for i in range(12):
                mm(ps2[:, 0:512], og_t[:, i, :], swo[:, i, 512:1024],
                   [og_b, swo_b], ps2_b, start=(i == 0), stop=(i == 11))
            em.op("vector", lambda e, jj=jj, ps=ps, yt=yt, xt=xt: e.tensor_tensor(
                out=yt[:, jj, 0:512], in0=ps[:, 0:512], in1=xt[:, 0:512], op=ALU.add),
                reads=[ps_b, xt_b], writes=[yt_b])
            em.op("vector", lambda e, jj=jj, ps2=ps2, yt=yt, xt=xt: e.tensor_tensor(
                out=yt[:, jj, 512:1024], in0=ps2[:, 0:512], in1=xt[:, 512:1024], op=ALU.add),
                reads=[ps2_b, xt_b], writes=[yt_b])
            if getattr(cx, "dbgy", None) is not None:
                em.dma("scalar", lambda e, jj=jj, tk=tk, yt=yt: e.dma_start(
                    out=cx.dbgy[tk:tk + 128, :], in_=yt[:, jj, :]),
                    reads=[yt_b], writes=[])
            em.op("vector", lambda e, jj=jj, yt=yt: e.tensor_reduce(
                out=mc2[:, 0:1], in_=yt[:, jj, :], axis=AX.X, op=ALU.add),
                reads=[yt_b], writes=[mc2_b])
            em.op("scalar", lambda e, jj=jj, yt=yt: e.activation(
                out=tscr[:], in_=yt[:, jj, :], func=AF.Square),
                reads=[yt_b], writes=[tscr_b])
            em.op("vector", lambda e: e.tensor_reduce(
                out=mc2[:, 1:2], in_=tscr[:], axis=AX.X, op=ALU.add),
                reads=[tscr_b], writes=[mc2_b])
            em.op("vector", lambda e: e.tensor_scalar(
                out=mc2[:, 2:3], in0=mc2[:, 0:1], scalar1=1.0 / D, scalar2=None,
                op0=ALU.mult), reads=[mc2_b], writes=[mc2_b])
            em.op("vector", lambda e: e.tensor_tensor(
                out=mc2[:, 3:4], in0=mc2[:, 2:3], in1=mc2[:, 2:3], op=ALU.mult),
                reads=[mc2_b], writes=[mc2_b])
            em.op("vector", lambda e: e.tensor_scalar(
                out=mc2[:, 4:5], in0=mc2[:, 1:2], scalar1=1.0 / D, scalar2=mc2[:, 3:4],
                op0=ALU.mult, op1=ALU.subtract), reads=[mc2_b], writes=[mc2_b])
            em.op("scalar", lambda e: e.activation(
                out=mc2[:, 5:6], in_=mc2[:, 4:5], func=AF.Ln, bias=cb[:, 0:1]),
                reads=[mc2_b, cb_b], writes=[mc2_b])
            em.op("scalar", lambda e: e.activation(
                out=mc2[:, 6:7], in_=mc2[:, 5:6], func=AF.Exp, scale=-0.5),
                reads=[mc2_b], writes=[mc2_b])
            em.op("vector", lambda e, jj=jj, yt=yt: e.tensor_scalar(
                out=x2b[:], in0=yt[:, jj, :], scalar1=mc2[:, 2:3], scalar2=mc2[:, 6:7],
                op0=ALU.subtract, op1=ALU.mult), reads=[yt_b, mc2_b], writes=[x2b_b])
            for ic in range(8):
                tr(psT[:], x2b[:, 128 * ic:128 * (ic + 1)], [x2b_b], psT_b)
                em.op("vector", lambda e, jj=jj, ic=ic, xf2=xf2: e.tensor_copy(
                    out=xf2[:, ic, 128 * jj:128 * (jj + 1)], in_=psT[:]),
                    reads=[psT_b], writes=[xf2_b])
        ps, ps_b = pp[0]
        for ic in range(8):
            mm(ps[0:16, 0:TT], svt[:, ic, :], xf2[:, ic, :], [svt_b, xf2_b], ps_b,
               start=(ic == 0), stop=(ic == 7))
        em.op("vector", lambda e, ps=ps: e.tensor_copy(out=c1[:], in_=ps[0:16, 0:TT]),
              reads=[ps_b], writes=[c1_b])
        mm(ps[0:16, 0:TT], sfbt[:], c1[0:16, :], [sfbt_b, c1_b], ps_b)
        em.op("vector", lambda e, ps=ps: e.tensor_copy(out=c2[:], in_=ps[0:16, 0:TT]),
              reads=[ps_b], writes=[c2_b])
        for jc in range(32):
            ps, ps_b = pp[jc % 2]
            for ic in range(8):
                wf, wf_b = wef[(jc * 8 + ic) % 2]
                em.dma("gpsimd", lambda e, jc=jc, ic=ic, wf=wf: e.dma_start(
                    out=wf[:], in_=wfc_sc[ic, jc, :, :]), reads=[cx.wfc_sc_b],
                    writes=[wf_b])
                mm(ps[:, 0:TT], wf[:], xf2[:, ic, :], [wf_b, xf2_b], ps_b,
                   start=(ic == 0), stop=False)
            mm(ps[:, 0:TT], sut[:, jc, :], c2[0:16, :], [sut_b, c2_b], ps_b,
               start=False, stop=True)
            em.op("vector", lambda e, jc=jc, ps=ps: e.tensor_scalar(
                out=r2t[:, jc, :], in0=ps[:, 0:TT], scalar1=0.0, scalar2=None,
                op0=ALU.max), reads=[ps_b], writes=[r2t_b])
            em.op("scalar", lambda e, jc=jc: e.activation(
                out=r2t[:, jc, :], in_=r2t[:, jc, :], func=AF.Square),
                reads=[r2t_b], writes=[r2t_b])
        if getattr(cx, "dbgr2", None) is not None and t5 == 1:
            r2d, r2d_b = cx.sb("r2d", (128, TT), F32)
            for jc in range(32):
                em.op("vector", lambda e, jc=jc: e.tensor_copy(
                    out=r2d[:], in_=r2t[:, jc, :]), reads=[r2t_b], writes=[r2d_b])
                em.dma("scalar", lambda e, jc=jc: e.dma_start(
                    out=cx.dbgr2[128 * jc:128 * (jc + 1), :], in_=r2d[:]),
                    reads=[r2d_b], writes=[])
        for dh in range(2):
            for jc in range(32):
                wp, wp_b = wpt[jc % 2]
                em.dma("gpsimd", lambda e, jc=jc, dh=dh, wp=wp: e.dma_start(
                    out=wp[:], in_=wpr_sc[128 * jc:128 * (jc + 1), 512 * dh:512 * (dh + 1)]),
                    reads=[cx.wpr_sc_b], writes=[wp_b])
                for jj in range(4):
                    ps, ps_b = pp[jj]
                    mm(ps[:, 0:512], r2t[:, jc, 128 * jj:128 * (jj + 1)], wp[:],
                       [r2t_b, wp_b], ps_b, start=(jc == 0), stop=(jc == 31))
            for jj in range(4):
                ps, ps_b = pp[jj]
                em.op("vector", lambda e, jj=jj, dh=dh, ps=ps, yt=yt: e.tensor_tensor(
                    out=outt[:, 512 * dh:512 * (dh + 1)], in0=ps[:, 0:512],
                    in1=yt[:, jj, 512 * dh:512 * (dh + 1)], op=ALU.add),
                    reads=[ps_b, yt_b], writes=[outt_b])
                tk = t5 * TT + jj * 128
                em.dma("scalar", lambda e, tk=tk, dh=dh: e.dma_start(
                    out=y_out[tk:tk + 128, 512 * dh:512 * (dh + 1)],
                    in_=outt[:, 512 * dh:512 * (dh + 1)]), reads=[outt_b], writes=[])

    for t5 in range(NT2 if nt2 is None else nt2):
        do_t5(t5, *yti[0], *x2f[0])



_ATTN_NC = None
_FFN_NC = None


def kernel(**inputs):
    global _ATTN_NC, _FFN_NC
    f32 = np.float32
    g = lambda n: np.ascontiguousarray(np.asarray(inputs[n], dtype=f32))
    x = g("L0_attn_x")
    Wq, Wk, Wv = g("L0_attn_Wq"), g("L0_attn_Wk"), g("L0_attn_Wv")
    Wb, Wa = g("L0_attn_Wb"), g("L0_attn_Wa")
    A_log, dt_bias = g("L0_attn_A_log"), g("L0_attn_dt_bias")
    conv_q, conv_k, conv_v = g("L0_attn_conv_q"), g("L0_attn_conv_k"), g("L0_attn_conv_v")
    Wg, norm_w, Wo = g("L0_attn_Wg"), g("L0_attn_norm_w"), g("L0_attn_Wo")
    fast_b, w_fc, w_proj = g("L0_ffn_fast_b"), g("L0_ffn_w_fc"), g("L0_ffn_w_proj")
    u, v, gate = g("L0_ffn_u"), g("L0_ffn_v"), g("L0_ffn_gate")

    consts = host_constants()
    if _ATTN_NC is None:
        _ATTN_NC = build_attn_program()
    if _FFN_NC is None:
        _FFN_NC = build_ffn_program()

    in_maps = []
    for c in range(8):
        b, hg = c // 2, c % 2
        hs = slice(384 * hg, 384 * (hg + 1))
        vs = slice(768 * hg, 768 * (hg + 1))
        h3 = slice(3 * hg, 3 * (hg + 1))
        wba = np.concatenate([Wb[:, h3], Wa[:, h3], np.zeros((D, 2), f32)], axis=1)
        aldt = np.concatenate([A_log[h3], dt_bias[h3], np.zeros(2, f32)])[None, :]
        m = dict(x=x[b], wq=Wq[:, hs], wk=Wk[:, hs], wv=Wv[:, vs], wg=Wg[:, vs],
                 wba=wba, convq=conv_q[hs], convk=conv_k[hs], convv=conv_v[vs],
                 aldt=aldt, **consts)
        in_maps.append({k: np.ascontiguousarray(v) for k, v in m.items()})
    res1 = run_bass_kernel_spmd(_ATTN_NC, in_maps, core_ids=list(range(8)))
    og = [res1.results[c]["og"] for c in range(8)]

    nw_rep = np.tile(norm_w, H)[:, None]
    in_maps2 = []
    for c in range(8):
        b, half = c // 2, c % 2
        ts = slice(2048 * half, 2048 * (half + 1))
        og_full = np.concatenate([og[2 * b][:, ts], og[2 * b + 1][:, ts]], axis=0)
        m = dict(x=x[b][ts], og=og_full, wo=Wo, nw=nw_rep, wfc=w_fc, wpr=w_proj,
                 lru=u, lrv=v, lrfb=fast_b, lrgate=gate.reshape(1, 1), **consts)
        in_maps2.append({k: np.ascontiguousarray(v) for k, v in m.items()})
    res2 = run_bass_kernel_spmd(_FFN_NC, in_maps2, core_ids=list(range(8)))
    out = np.empty((B, L, D), np.float32)
    for c in range(8):
        b, half = c // 2, c % 2
        out[b, 2048 * half:2048 * (half + 1), :] = res2.results[c]["y"]
    return out


# version 03 (final)
# speedup vs baseline: 1.0000x; 1.0000x over previous
import numpy as np
from contextlib import ExitStack

import concourse.bass as bass
import concourse.mybir as mybir
from concourse.bass_utils import run_bass_kernel_spmd

F32 = mybir.dt.float32
BF16 = mybir.dt.bfloat16
AF = mybir.ActivationFunctionType
ALU = mybir.AluOpType
AX = mybir.AxisListType

B, L, D = 4, 4096, 1024
H, DK, DV = 6, 128, 256
KD3, VD3 = 384, 768
C = 128
NCH = L // C
FH, RK = 4096, 16
EPS = 1e-5
TT = 512

STREAMS = ("tensor", "vector", "scalar", "gpsimd", "sync")


class Buf:
    __slots__ = ("name", "writer", "readers")

    def __init__(self, name):
        self.name = name
        self.writer = None
        self.readers = []


class Emitter:
    def __init__(self):
        self.ops = []
        self.cnt = {(s, k): 0 for s in STREAMS for k in ("c", "d")}

    def _op(self, stream, kind, fn, reads=(), writes=()):
        deps = []
        for b in reads:
            if b.writer is not None:
                deps.append(b.writer)
        for b in writes:
            if b.writer is not None:
                deps.append(b.writer)
            deps.extend(b.readers)
        self.cnt[(stream, kind)] += 1
        tok = (stream, kind, self.cnt[(stream, kind)])
        if stream == "tensor" or kind == "d":
            waits = [d for d in deps if (d[0], d[1]) != (stream, kind)]
        else:
            waits = list(deps)
        self.ops.append((stream, kind, fn, waits))
        for b in reads:
            b.readers.append(tok)
        for b in writes:
            b.writer = tok
            b.readers = []
        return tok

    def op(self, stream, fn, reads=(), writes=()):
        return self._op(stream, "c", fn, reads, writes)

    def dma(self, stream, fn, reads=(), writes=()):
        assert stream in ("sync", "scalar", "gpsimd")
        return self._op(stream, "d", fn, reads, writes)

    def emit(self, nc, block, sems):
        streams = {s: [(k, fn, w) for (ss, k, fn, w) in self.ops if ss == s]
                   for s in STREAMS}

        def run_stream(eng, s):
            hw = {}
            ndma = 0
            for kind, fn, waits in streams[s]:
                if kind == "d" and ndma:
                    if hw.get((s, "d"), 0) < 16 * ndma:
                        hw[(s, "d")] = 16 * ndma
                        eng.wait_ge(sems[(s, "d")], 16 * ndma)
                for (ws, wk, wc) in waits:
                    val = wc * (16 if wk == "d" else 1)
                    if hw.get((ws, wk), 0) < val:
                        hw[(ws, wk)] = val
                        eng.wait_ge(sems[(ws, wk)], val)
                if kind == "d":
                    ndma += 1
                fn(eng).then_inc(sems[(s, kind)], 16 if kind == "d" else 1)
            if ndma:
                eng.wait_ge(sems[(s, "d")], 16 * ndma)

        @block.tensor
        def _(eng):
            run_stream(eng, "tensor")

        @block.vector
        def _(eng):
            run_stream(eng, "vector")

        @block.scalar
        def _(eng):
            run_stream(eng, "scalar")

        @block.gpsimd
        def _(eng):
            run_stream(eng, "gpsimd")

        @block.sync
        def _(eng):
            run_stream(eng, "sync")


class Ctx:
    def __init__(self, nc, em, st):
        self.nc = nc
        self.em = em
        self.st = st

    def sb(self, name, shape, dtype):
        t = self.st.enter_context(self.nc.sbuf_tensor(name, list(shape), dtype))
        return t, Buf(name)

    def ps(self, name, shape, dtype=F32):
        t = self.st.enter_context(self.nc.psum_tensor(name, list(shape), dtype))
        return t, Buf(name)


def host_constants():
    sel_last = np.zeros((128, 128), np.float32)
    sel_last[127, :] = 1.0
    sel3 = np.zeros((4, 384), np.float32)
    for h in range(3):
        sel3[h, 128 * h:128 * (h + 1)] = 1.0
    return dict(
        c_sel3=sel3,
        c_ident=np.eye(128, dtype=np.float32),
        c_triu=np.triu(np.ones((128, 128), np.float32)),
        c_trils=np.tril(np.ones((128, 128), np.float32), -1),
        c_sel_last=sel_last,
        c_onescol=np.ones((128, 1), np.float32),
        c_onesrow=np.ones((1, 128), np.float32),
    )


def load_constants(cx, em):
    nc = cx.nc
    cc = {}
    specs = [("c_sel3", (4, 384), F32),
             ("c_ident", (128, 128), BF16), ("c_triu", (128, 128), F32),
             ("c_trils", (128, 128), BF16), ("c_sel_last", (128, 128), F32),
             ("c_onescol", (128, 1), BF16), ("c_onesrow", (1, 128), BF16)]
    stg, stg_b = cx.sb("cstg", (128, 128), F32)
    for name, shape, dt_sb in specs:
        dr = nc.dram_tensor(name, list(shape), F32, kind="ExternalInput").ap()
        t, b = cx.sb("s" + name, shape, dt_sb)
        if dt_sb == F32:
            em.dma("sync", lambda e, t=t, dr=dr: e.dma_start(out=t[:], in_=dr), writes=[b])
        else:
            em.dma("sync", lambda e, dr=dr, shape=shape: e.dma_start(
                out=stg[0:shape[0], 0:shape[1]], in_=dr), writes=[stg_b])
            em.op("vector", lambda e, t=t, shape=shape: e.tensor_copy(
                out=t[:], in_=stg[0:shape[0], 0:shape[1]]), reads=[stg_b], writes=[b])
        cc[name] = (t, b)
    t, b = cx.sb("s_triu_bf", (128, 128), BF16)
    em.op("vector", lambda e, t=t: e.tensor_copy(out=t[:], in_=cc["c_triu"][0][:]),
          reads=[cc["c_triu"][1]], writes=[b])
    cc["c_triu_bf"] = (t, b)
    t, b = cx.sb("s_ident_f", (128, 128), F32)
    em.op("vector", lambda e, t=t: e.tensor_copy(out=t[:], in_=cc["c_ident"][0][:]),
          reads=[cc["c_ident"][1]], writes=[b])
    cc["c_ident_f"] = (t, b)
    cb, cb_b = cx.sb("cbias", (128, 4), F32)
    for i, val in enumerate([1e-5, 1e-12, -0.5 * float(np.log(128.0)), 1.0]):
        em.op("vector", lambda e, i=i, val=val: e.memset(cb[:, i:i + 1], val), writes=[cb_b])
    cc["cbias"] = (cb, cb_b)
    return cc



def build_attn_program(n_chunks=NCH, debug=False):
    nc = bass.Bass("TRN2", target_bir_lowering=False)
    em = Emitter()
    with ExitStack() as st:
        cx = Ctx(nc, em, st)
        cx.debug = debug
        P = n_chunks * C
        x_in = nc.dram_tensor("x", [P, D], F32, kind="ExternalInput").ap()
        wq_in = nc.dram_tensor("wq", [D, KD3], F32, kind="ExternalInput").ap()
        wk_in = nc.dram_tensor("wk", [D, KD3], F32, kind="ExternalInput").ap()
        wv_in = nc.dram_tensor("wv", [D, VD3], F32, kind="ExternalInput").ap()
        wg_in = nc.dram_tensor("wg", [D, VD3], F32, kind="ExternalInput").ap()
        wba_in = nc.dram_tensor("wba", [D, 8], F32, kind="ExternalInput").ap()
        cq_in = nc.dram_tensor("convq", [KD3, 4], F32, kind="ExternalInput").ap()
        ck_in = nc.dram_tensor("convk", [KD3, 4], F32, kind="ExternalInput").ap()
        cv_in = nc.dram_tensor("convv", [VD3, 4], F32, kind="ExternalInput").ap()
        aldt_in = nc.dram_tensor("aldt", [1, 8], F32, kind="ExternalInput").ap()
        og_out = nc.dram_tensor("og", [VD3, P], F32, kind="ExternalOutput").ap()
        if debug:
            cx.dbg_out = nc.dram_tensor("dbg", [128, 4096], F32, kind="ExternalOutput").ap()

        cc = load_constants(cx, em)
        build_attn_body(cx, em, cc, x_in, wq_in, wk_in, wv_in, wg_in, wba_in,
                        cq_in, ck_in, cv_in, aldt_in, og_out, n_chunks)
        sems = {}
        with ExitStack() as st2:
            for s in STREAMS:
                for k in ("c", "d"):
                    sems[(s, k)] = st2.enter_context(nc.semaphore(f"sem_{s}_{k}"))
            block = st2.enter_context(nc.Block())
            em.emit(nc, block, sems)
    return nc


def build_attn_body(cx, em, cc, x_in, wq_in, wk_in, wv_in, wg_in, wba_in,
                    cq_in, ck_in, cv_in, aldt_in, og_out, n_chunks):
    nc = cx.nc
    ident, ident_b = cc["c_ident"]
    identf, identf_b = cc["c_ident_f"]
    triu, triu_b = cc["c_triu"]
    triubf, triubf_b = cc["c_triu_bf"]
    trils, trils_b = cc["c_trils"]
    sel_last, sel_last_b = cc["c_sel_last"]
    sel3, sel3_b = cc["c_sel3"]
    cb, cb_b = cc["cbias"]
    onescol, onescol_b = cc["c_onescol"]
    onesrow, onesrow_b = cc["c_onesrow"]
    P = n_chunks * C
    NT = P // TT

    def mm(out_ap, lhsT_ap, rhs_ap, reads, wbuf, start=True, stop=True):
        em.op("tensor", lambda e: e.matmul(out=out_ap, lhsT=lhsT_ap, rhs=rhs_ap,
                                           start=start, stop=stop),
              reads=list(reads), writes=[wbuf])

    def tr(out_ap, in_ap, reads, wbuf, idt=None, idb=None, p=128):
        it = ident if idt is None else idt
        ib = ident_b if idb is None else idb
        em.op("tensor", lambda e: e.transpose(out=out_ap, in_=in_ap, identity=it[0:p, 0:p]),
              reads=list(reads) + [ib], writes=[wbuf])

    def load_weight(name, dr, rows, cols):
        nch = rows // 128
        stg, stg_b = cx.sb(name + "_stg", (128, cols), F32)
        wt, wt_b = cx.sb(name, (128, nch, cols), BF16)
        for i in range(nch):
            em.dma("sync", lambda e, i=i: e.dma_start(
                out=stg[:], in_=dr[128 * i:128 * (i + 1), :]), writes=[stg_b])
            em.op("vector", lambda e, i=i: e.tensor_copy(out=wt[:, i, :], in_=stg[:]),
                  reads=[stg_b], writes=[wt_b])
        return wt, wt_b

    swq, swq_b = load_weight("swq", wq_in, D, KD3)
    swk, swk_b = load_weight("swk", wk_in, D, KD3)
    swv, swv_b = load_weight("swv", wv_in, D, VD3)
    swg, swg_b = load_weight("swg", wg_in, D, VD3)
    swba, swba_b = load_weight("swba", wba_in, D, 8)

    def load_conv(name, dr, rows):
        nch = rows // 128
        t, b = cx.sb(name, (128, nch, 4), F32)
        for i in range(nch):
            em.dma("sync", lambda e, i=i: e.dma_start(
                out=t[:, i, :], in_=dr[128 * i:128 * (i + 1), :]), writes=[b])
        return t, b

    scq, scq_b = load_conv("scq", cq_in, KD3)
    sck, sck_b = load_conv("sck", ck_in, KD3)
    scv, scv_b = load_conv("scv", cv_in, VD3)

    aldt, aldt_b = cx.sb("s_aldt", (1, 8), F32)
    em.dma("sync", lambda e: e.dma_start(out=aldt[:], in_=aldt_in), writes=[aldt_b])
    onesrow_f, onesrow_f_b = cx.sb("onesrow_f", (1, 128), F32)
    em.op("vector", lambda e: e.memset(onesrow_f[:], 1.0), writes=[onesrow_f_b])
    nalog, nalog_b = cx.sb("nalog", (1, 3), F32)
    em.op("scalar", lambda e: e.activation(out=nalog[:], in_=aldt[0:1, 0:3], func=AF.Exp),
          reads=[aldt_b], writes=[nalog_b])
    em.op("vector", lambda e: e.tensor_scalar(out=nalog[:], in0=nalog[:], scalar1=-1.0,
                                              scalar2=None, op0=ALU.mult),
          reads=[nalog_b], writes=[nalog_b])
    dtb_bf, dtb_bf_b = cx.sb("dtb_bf", (1, 8), BF16)
    em.op("vector", lambda e: e.memset(dtb_bf[:], 0.0), writes=[dtb_bf_b])
    em.op("vector", lambda e: e.tensor_copy(out=dtb_bf[0:1, 3:6], in_=aldt[0:1, 3:6]),
          reads=[aldt_b], writes=[dtb_bf_b])

    pp = [cx.ps(f"pp{i}", (128, 512), F32) for i in range(2)]
    psU, psU_b = cx.ps("psU", (128, 512), F32)
    psV, psV_b = cx.ps("psV", (128, 512), F32)
    psA, psA_b = cx.ps("psA", (128, 128), F32)
    psB, psB_b = cx.ps("psB", (128, 128), F32)
    psT, psT_b = cx.ps("psT", (128, 128), BF16)
    psF, psF_b = cx.ps("psF", (128, 136), F32)

    nalog_bc, nalog_bc_b = cx.sb("nalog_bc", (128, 3), F32)
    mm(psF[:, 128:131], onesrow_f[:], nalog[:], [onesrow_f_b, nalog_b], psF_b)
    em.op("vector", lambda e: e.tensor_copy(out=nalog_bc[:], in_=psF[:, 128:131]),
          reads=[psF_b], writes=[nalog_bc_b])

    xst = [cx.sb(f"xst{i}", (128, 4, D), F32) for i in range(2)]
    xbt = [cx.sb(f"xbt{i}", (128, D), BF16) for i in range(2)]
    xfm = [cx.sb(f"xfm{i}", (128, 8, TT), BF16) for i in range(2)]
    pstg, pstg_b = cx.sb("pstg", (128, 12, TT + 3), BF16)
    qkv = [cx.sb(f"qkv{i}", (128, 12, TT), BF16) for i in range(2)]
    gat = [cx.sb(f"gat{i}", (128, 4, VD3), BF16) for i in range(2)]
    cacc = [cx.sb(f"cacc{i}", (128, TT), F32) for i in range(2)]

    nring = 4
    bcol = [cx.sb(f"bcol{i}", (128, 3), F32) for i in range(nring)]
    becc = [cx.sb(f"becc{i}", (128, 3), F32) for i in range(nring)]
    ecum = [cx.sb(f"ecum{i}", (128, 3), F32) for i in range(nring)]
    cumc = [cx.sb(f"cumc{i}", (128, 3), F32) for i in range(nring)]
    cumt = [cx.sb(f"cumt{i}", (4, 128), F32) for i in range(nring)]
    eccb = [cx.sb(f"eccb{i}", (128, 3), F32) for i in range(nring)]
    efac = [cx.sb(f"efac{i}", (128, 3), F32) for i in range(nring)]
    u1r = [[cx.sb(f"u1_{i}_{h}", (128, DV), BF16) for h in range(3)] for i in range(nring)]
    w1t = [[cx.sb(f"w1t_{i}_{h}", (128, C), BF16) for h in range(3)] for i in range(nring)]
    mtr = [[cx.sb(f"mt_{i}_{h}", (128, C), BF16) for h in range(3)] for i in range(nring)]
    khr = [[cx.sb(f"kh_{i}_{h}", (128, DK), BF16) for h in range(3)] for i in range(nring)]
    sqr = [[cx.sb(f"sq_{i}_{h}", (128, 2), F32) for h in range(3)] for i in range(nring)]

    Sf = [[cx.sb(f"Sf{h}_{p}", (128, DV), F32) for p in range(2)] for h in range(3)]
    Sb = [cx.sb(f"Sb{h}", (128, DV), BF16) for h in range(3)]
    for h in range(3):
        em.op("vector", lambda e, h=h: e.memset(Sf[h][0][0][:], 0.0), writes=[Sf[h][0][1]])
        em.op("vector", lambda e, h=h: e.memset(Sb[h][0][:], 0.0), writes=[Sb[h][1]])

    mcol, mcol_b = cx.sb("mcol", (128, 8), F32)
    lnscr, lnscr_b = cx.sb("lnscr", (128, D), F32)
    rsq, rsq_b = cx.sb("rsq", (128, 4), F32)
    rowb, rowb_b = cx.sb("rowb", (1, 128), F32)
    sqscr, sqscr_b = cx.sb("sqscr", (128, 128), BF16)
    dmat, dmat_b = cx.sb("dmat", (128, C), F32)
    dmat2, dmat2_b = cx.sb("dmat2", (128, C), F32)
    cumbc, cumbc_b = cx.sb("cumbc", (128, C), F32)
    zt = [cx.sb(f"zt{h}", (128, C), BF16) for h in range(3)]
    pw = [cx.sb(f"pw{h}", (128, C), BF16) for h in range(3)]
    pwt = [cx.sb(f"pwt{h}", (128, C), BF16) for h in range(3)]
    amat = [cx.sb(f"amat{h}", (128, C), BF16) for h in range(3)]
    bmask, bmask_b = cx.sb("bmask", (128, C), BF16)
    ktm, ktm_b = cx.sb("ktm", (128, DK), BF16)
    kbtm = [cx.sb(f"kbtm{h}", (128, DK), BF16) for h in range(3)]
    vtm = [cx.sb(f"vtm{h}", (128, DV), BF16) for h in range(3)]
    oscr, oscr_b = cx.sb("oscr", (128, DV), F32)
    oscr2, oscr2_b = cx.sb("oscr2", (128, DV), F32)
    ogb, ogb_b = cx.sb("ogb", (128, DV), BF16)
    ogt, ogt_b = cx.sb("ogt", (128, 2, 128), F32)

    em.op("vector", lambda e: e.memset(pstg[:], 0.0), writes=[pstg_b])

    def phase_a(c, slot, j, qk, qk_b, gt_t, gt_b):
        bc, bc_b = bcol[slot]
        bec, bec_b = becc[slot]
        ec, ec_b = ecum[slot]
        cm, cm_b = cumc[slot]
        ct, ct_b = cumt[slot]
        eb, eb_b = eccb[slot]
        ef, ef_b = efac[slot]
        tc = slice(128 * j, 128 * (j + 1))
        for h in range(3):
            q_ap = qk[:, h, tc]
            k_ap = qk[:, 3 + h, tc]
            v0_ap = qk[:, 6 + 2 * h, tc]
            v1_ap = qk[:, 7 + 2 * h, tc]
            kbtm_t, kbtm_b = kbtm[h]
            vtm_t, vtm_b = vtm[h]
            zt_t, zt_b = zt[h]
            pw_t, pw_b = pw[h]
            pwt_t, pwt_b = pwt[h]
            am_t, am_b = amat[h]
            u1_t, u1_b = u1r[slot][h]
            w1_t, w1_b = w1t[slot][h]
            mt_t, mt_b = mtr[slot][h]
            kh_t, kh_b = khr[slot][h]
            sq_t, sq_b = sqr[slot][h]

            tr(psT[:], k_ap, [qk_b], psT_b)
            em.op("scalar", lambda e: e.activation(out=ktm[:], in_=psT[:], func=AF.Identity),
                  reads=[psT_b], writes=[ktm_b])
            em.op("scalar", lambda e: e.activation(out=sqscr[:], in_=psT[:], func=AF.Square),
                  reads=[psT_b], writes=[sqscr_b])
            em.op("vector", lambda e: e.tensor_reduce(
                out=rsq[:, 0:1], in_=sqscr[:], axis=AX.X, op=ALU.add),
                reads=[sqscr_b], writes=[rsq_b])
            em.op("scalar", lambda e: e.activation(
                out=rsq[:, 1:2], in_=rsq[:, 0:1], func=AF.Ln, bias=cb[:, 1:2]),
                reads=[rsq_b, cb_b], writes=[rsq_b])
            em.op("scalar", lambda e: e.activation(
                out=rsq[:, 1:2], in_=rsq[:, 1:2], func=AF.Exp, scale=-0.5),
                reads=[rsq_b], writes=[rsq_b])
            em.op("scalar", lambda e: e.activation(
                out=ktm[:], in_=ktm[:], func=AF.Identity, scale=rsq[:, 1:2]),
                reads=[ktm_b, rsq_b], writes=[ktm_b])
            tr(psT[:], ktm[:], [ktm_b], psT_b)
            em.op("vector", lambda e, k_ap=k_ap: e.tensor_copy(out=k_ap, in_=psT[:]),
                  reads=[psT_b], writes=[qk_b])
            em.op("scalar", lambda e, kbtm_t=kbtm_t, h=h: e.activation(
                out=kbtm_t[:], in_=ktm[:], func=AF.Identity, scale=bec[:, h:h + 1]),
                reads=[ktm_b, bec_b], writes=[kbtm_b])
            em.op("scalar", lambda e, kh_t=kh_t, h=h: e.activation(
                out=kh_t[:], in_=ktm[:], func=AF.Identity, scale=ef[:, h:h + 1]),
                reads=[ktm_b, ef_b], writes=[kh_b])
            tr(psT[:], v0_ap, [qk_b], psT_b)
            em.op("scalar", lambda e, vtm_t=vtm_t, h=h: e.activation(
                out=vtm_t[:, 0:128], in_=psT[:], func=AF.Identity, scale=bc[:, h:h + 1]),
                reads=[psT_b, bc_b], writes=[vtm_b])
            tr(psT[:], v1_ap, [qk_b], psT_b)
            em.op("scalar", lambda e, vtm_t=vtm_t, h=h: e.activation(
                out=vtm_t[:, 128:256], in_=psT[:], func=AF.Identity, scale=bc[:, h:h + 1]),
                reads=[psT_b, bc_b], writes=[vtm_b])
            em.op("scalar", lambda e, q_ap=q_ap: e.activation(
                out=sqscr[:], in_=q_ap, func=AF.Square), reads=[qk_b], writes=[sqscr_b])
            mm(psU[0:1, 0:128], onescol[:], sqscr[:], [onescol_b, sqscr_b], psU_b)
            em.op("vector", lambda e: e.tensor_copy(out=rowb[:], in_=psU[0:1, 0:128]),
                  reads=[psU_b], writes=[rowb_b])
            tr(psF[:, 0:1], rowb[:], [rowb_b], psF_b, idt=identf, idb=identf_b, p=1)
            em.op("scalar", lambda e: e.activation(
                out=rsq[:, 2:3], in_=psF[:, 0:1], func=AF.Ln, bias=cb[:, 1:2]),
                reads=[psF_b, cb_b], writes=[rsq_b])
            em.op("scalar", lambda e: e.activation(
                out=rsq[:, 2:3], in_=rsq[:, 2:3], func=AF.Exp, scale=-0.5,
                bias=cb[:, 2:3]),
                reads=[rsq_b, cb_b], writes=[rsq_b])
            em.op("vector", lambda e, sq_t=sq_t: e.tensor_copy(
                out=sq_t[:, 0:1], in_=rsq[:, 2:3]), reads=[rsq_b], writes=[sq_b])
            em.op("vector", lambda e, sq_t=sq_t, h=h: e.tensor_tensor(
                out=sq_t[:, 1:2], in0=rsq[:, 2:3], in1=ec[:, h:h + 1], op=ALU.mult),
                reads=[rsq_b, ec_b], writes=[sq_b])
            mm(psU[:, 0:128], sel3[0:3, 128 * h:128 * (h + 1)], ct[0:3, :],
               [ct_b, sel3_b], psU_b)
            em.op("vector", lambda e: e.tensor_copy(out=cumbc[:], in_=psU[:, 0:128]),
                  reads=[psU_b], writes=[cumbc_b])
            em.op("vector", lambda e, h=h: e.tensor_scalar(
                out=dmat[:], in0=cumbc[:], scalar1=-1.0, scalar2=cm[:, h:h + 1],
                op0=ALU.mult, op1=ALU.add),
                reads=[cumbc_b, cm_b], writes=[dmat_b])
            em.op("vector", lambda e: e.tensor_scalar(
                out=dmat[:], in0=dmat[:], scalar1=0.0, scalar2=None, op0=ALU.min),
                reads=[dmat_b], writes=[dmat_b])
            em.op("scalar", lambda e: e.activation(out=dmat[:], in_=dmat[:], func=AF.Exp),
                  reads=[dmat_b], writes=[dmat_b])
            em.op("vector", lambda e, h=h: e.tensor_scalar(
                out=dmat2[:], in0=cumbc[:], scalar1=cm[:, h:h + 1], scalar2=0.0,
                op0=ALU.subtract, op1=ALU.min),
                reads=[cumbc_b, cm_b], writes=[dmat2_b])
            em.op("scalar", lambda e: e.activation(out=dmat2[:], in_=dmat2[:], func=AF.Exp),
                  reads=[dmat2_b], writes=[dmat2_b])
            mm(psA[:], k_ap, k_ap, [qk_b], psA_b)
            em.op("scalar", lambda e, h=h: e.activation(
                out=bmask[:], in_=trils[:], func=AF.Identity, scale=bc[:, h:h + 1]),
                reads=[trils_b, bc_b], writes=[bmask_b])
            em.op("vector", lambda e: e.tensor_tensor(
                out=dmat[:], in0=psA[:], in1=dmat[:], op=ALU.mult),
                reads=[psA_b, dmat_b], writes=[dmat_b])
            em.op("vector", lambda e, am_t=am_t: e.tensor_tensor(
                out=am_t[:], in0=dmat[:], in1=bmask[:], op=ALU.mult),
                reads=[dmat_b, bmask_b], writes=[am_b])
            mm(psB[:], k_ap, q_ap, [qk_b], psB_b)
            em.op("vector", lambda e: e.tensor_tensor(
                out=dmat2[:], in0=psB[:], in1=dmat2[:], op=ALU.mult),
                reads=[psB_b, dmat2_b], writes=[dmat2_b])
            em.op("vector", lambda e, mt_t=mt_t: e.tensor_tensor(
                out=mt_t[:], in0=dmat2[:], in1=triubf[:], op=ALU.mult),
                reads=[dmat2_b, triubf_b], writes=[mt_b])
            tr(psT[:], am_t[:], [am_b], psT_b)
            em.op("vector", lambda e, pwt_t=pwt_t: e.tensor_copy(out=pwt_t[:], in_=psT[:]),
                  reads=[psT_b], writes=[pwt_b])
            em.op("vector", lambda e, zt_t=zt_t, pwt_t=pwt_t: e.tensor_tensor(
                out=zt_t[:], in0=ident[:], in1=pwt_t[:], op=ALU.subtract),
                reads=[ident_b, pwt_b], writes=[zt_b])
            em.op("vector", lambda e, pw_t=pw_t, am_t=am_t: e.tensor_copy(
                out=pw_t[:], in_=am_t[:]), reads=[am_b], writes=[pw_b])
            for lev in range(6):
                mm(psA[:], pwt_t[:], pw_t[:], [pwt_b, pw_b], psA_b)
                if lev < 5:
                    mm(psB[:], pw_t[:], pwt_t[:], [pw_b, pwt_b], psB_b)
                em.op("vector", lambda e, pw_t=pw_t: e.tensor_copy(out=pw_t[:], in_=psA[:]),
                      reads=[psA_b], writes=[pw_b])
                if lev < 5:
                    em.op("vector", lambda e, pwt_t=pwt_t: e.tensor_copy(
                        out=pwt_t[:], in_=psB[:]), reads=[psB_b], writes=[pwt_b])
                mm(psA[:], pw_t[:], zt_t[:], [pw_b, zt_b], psA_b)
                em.op("vector", lambda e, zt_t=zt_t: e.tensor_tensor(
                    out=zt_t[:], in0=psA[:], in1=zt_t[:], op=ALU.add),
                    reads=[psA_b, zt_b], writes=[zt_b])
            mm(psU[:, 0:DV], zt_t[:], vtm_t[:], [zt_b, vtm_b], psU_b)
            em.op("vector", lambda e, u1_t=u1_t: e.tensor_copy(out=u1_t[:], in_=psU[:, 0:DV]),
                  reads=[psU_b], writes=[u1_b])
            mm(psB[:], kbtm_t[:], zt_t[:], [kbtm_b, zt_b], psB_b)
            em.op("vector", lambda e, w1_t=w1_t: e.tensor_copy(out=w1_t[:], in_=psB[:]),
                  reads=[psB_b], writes=[w1_b])

    def phase_b(c, slot, j, qk, qk_b, gt_t, gt_b):
        eb, eb_b = eccb[slot]
        tc = slice(128 * j, 128 * (j + 1))
        for h in range(3):
            q_ap = qk[:, h, tc]
            u1_t, u1_b = u1r[slot][h]
            w1_t, w1_b = w1t[slot][h]
            mt_t, mt_b = mtr[slot][h]
            kh_t, kh_b = khr[slot][h]
            sq_t, sq_b = sqr[slot][h]
            sf_t, sf_b = Sf[h][c % 2]
            sf2_t, sf2_b = Sf[h][(c + 1) % 2]
            sb_t, sb_b = Sb[h]
            mm(psU[:, 0:DV], w1_t[:], sb_t[:], [w1_b, sb_b], psU_b)
            em.op("vector", lambda e, u1_t=u1_t: e.tensor_tensor(
                out=u1_t[:], in0=u1_t[:], in1=psU[:, 0:DV], op=ALU.subtract),
                reads=[u1_b, psU_b], writes=[u1_b])
            mm(psV[:, 0:DV], q_ap, sb_t[:], [qk_b, sb_b], psV_b)
            em.op("scalar", lambda e, sq_t=sq_t: e.activation(
                out=oscr[:], in_=psV[:, 0:DV], func=AF.Identity, scale=sq_t[:, 1:2]),
                reads=[psV_b, sq_b], writes=[oscr_b])
            mm(psU[:, 0:DV], mt_t[:], u1_t[:], [mt_b, u1_b], psU_b)
            em.op("vector", lambda e, sq_t=sq_t: e.scalar_tensor_tensor(
                out=oscr[:], in0=psU[:, 0:DV], scalar=sq_t[:, 0:1], in1=oscr[:],
                op0=ALU.mult, op1=ALU.add),
                reads=[psU_b, sq_b, oscr_b], writes=[oscr_b])
            em.op("vector", lambda e: e.tensor_tensor(
                out=oscr2[:], in0=oscr[:], in1=oscr[:], op=ALU.mult),
                reads=[oscr_b], writes=[oscr2_b])
            em.op("vector", lambda e: e.tensor_reduce(
                out=rsq[:, 0:1], in_=oscr2[:], axis=AX.X, op=ALU.add),
                reads=[oscr2_b], writes=[rsq_b])
            em.op("scalar", lambda e: e.activation(
                out=rsq[:, 1:2], in_=rsq[:, 0:1], func=AF.Ln, scale=1.0 / DV,
                bias=cb[:, 0:1]),
                reads=[rsq_b, cb_b], writes=[rsq_b])
            em.op("scalar", lambda e: e.activation(
                out=rsq[:, 1:2], in_=rsq[:, 1:2], func=AF.Exp, scale=-0.5),
                reads=[rsq_b], writes=[rsq_b])
            em.op("vector", lambda e, h=h, gt_t=gt_t, j=j: e.scalar_tensor_tensor(
                out=ogb[:], in0=oscr[:], scalar=rsq[:, 1:2],
                in1=gt_t[:, j, 256 * h:256 * (h + 1)], op0=ALU.mult, op1=ALU.mult),
                reads=[oscr_b, rsq_b, gt_b], writes=[ogb_b])
            mm(psV[:, 0:DV], kh_t[:], u1_t[:], [kh_b, u1_b], psV_b)
            em.op("vector", lambda e, sf_t=sf_t, sf2_t=sf2_t, h=h: e.scalar_tensor_tensor(
                out=sf2_t[:], in0=sf_t[:], scalar=eb[:, h:h + 1], in1=psV[:, 0:DV],
                op0=ALU.mult, op1=ALU.add),
                reads=[sf_b, eb_b, psV_b], writes=[sf2_b])
            em.op("scalar", lambda e, sf2_t=sf2_t, sb_t=sb_t: e.activation(
                out=sb_t[:], in_=sf2_t[:], func=AF.Identity),
                reads=[sf2_b], writes=[sb_b])
            for half in range(2):
                tr(psT[:], ogb[:, 128 * half:128 * (half + 1)], [ogb_b], psT_b)
                em.op("vector", lambda e, half=half: e.tensor_copy(
                    out=ogt[:, half, :], in_=psT[:]), reads=[psT_b], writes=[ogt_b])
            for half in range(2):
                r0 = 256 * h + 128 * half
                em.dma("scalar", lambda e, half=half, r0=r0, c=c: e.dma_start(
                    out=og_out[r0:r0 + 128, 128 * c:128 * (c + 1)], in_=ogt[:, half, :]),
                    reads=[ogt_b], writes=[])

    def do_mt(mt, xs, xs_b, xb, xb_b, xf, xf_b, qk, qk_b, gt_t, gt_b):
        for jj in range(4):
            r0 = mt * TT + jj * 128
            em.dma("sync", lambda e, jj=jj, r0=r0: e.dma_start(
                out=xs[:, jj, :], in_=x_in[r0:r0 + 128, :]), writes=[xs_b])
        for jj in range(4):
            em.op("vector", lambda e, jj=jj: e.tensor_reduce(
                out=mcol[:, 0:1], in_=xs[:, jj, :], axis=AX.X, op=ALU.add),
                reads=[xs_b], writes=[mcol_b])
            em.op("scalar", lambda e, jj=jj: e.activation(
                out=lnscr[:], in_=xs[:, jj, :], func=AF.Square),
                reads=[xs_b], writes=[lnscr_b])
            em.op("vector", lambda e: e.tensor_reduce(
                out=mcol[:, 1:2], in_=lnscr[:], axis=AX.X, op=ALU.add),
                reads=[lnscr_b], writes=[mcol_b])
            em.op("vector", lambda e: e.tensor_scalar(
                out=mcol[:, 2:3], in0=mcol[:, 0:1], scalar1=1.0 / D, scalar2=None,
                op0=ALU.mult), reads=[mcol_b], writes=[mcol_b])
            em.op("vector", lambda e: e.tensor_tensor(
                out=mcol[:, 3:4], in0=mcol[:, 2:3], in1=mcol[:, 2:3], op=ALU.mult),
                reads=[mcol_b], writes=[mcol_b])
            em.op("vector", lambda e: e.tensor_scalar(
                out=mcol[:, 4:5], in0=mcol[:, 1:2], scalar1=1.0 / D, scalar2=mcol[:, 3:4],
                op0=ALU.mult, op1=ALU.subtract), reads=[mcol_b], writes=[mcol_b])
            em.op("scalar", lambda e: e.activation(
                out=mcol[:, 5:6], in_=mcol[:, 4:5], func=AF.Ln, bias=cb[:, 0:1]),
                reads=[mcol_b, cb_b], writes=[mcol_b])
            em.op("scalar", lambda e: e.activation(
                out=mcol[:, 6:7], in_=mcol[:, 5:6], func=AF.Exp, scale=-0.5),
                reads=[mcol_b], writes=[mcol_b])
            em.op("vector", lambda e, jj=jj: e.tensor_scalar(
                out=xb[:], in0=xs[:, jj, :], scalar1=mcol[:, 2:3], scalar2=mcol[:, 6:7],
                op0=ALU.subtract, op1=ALU.mult),
                reads=[xs_b, mcol_b], writes=[xb_b])
            for ic in range(8):
                tr(psT[:], xb[:, 128 * ic:128 * (ic + 1)], [xb_b], psT_b)
                em.op("vector", lambda e, jj=jj, ic=ic: e.tensor_copy(
                    out=xf[:, ic, 128 * jj:128 * (jj + 1)], in_=psT[:]),
                    reads=[psT_b], writes=[xf_b])
        em.op("vector", lambda e: e.tensor_copy(out=pstg[:, :, 0:3], in_=pstg[:, :, TT:TT + 3]),
              reads=[pstg_b], writes=[pstg_b])
        for oc in range(12):
            if oc < 3:
                wsrc, wsrc_b, off = swq, swq_b, oc
            elif oc < 6:
                wsrc, wsrc_b, off = swk, swk_b, oc - 3
            else:
                wsrc, wsrc_b, off = swv, swv_b, oc - 6
            ps, ps_b = pp[oc % 2]
            for ic in range(8):
                mm(ps[:, 0:TT], wsrc[:, ic, 128 * off:128 * (off + 1)], xf[:, ic, :],
                   [wsrc_b, xf_b], ps_b, start=(ic == 0), stop=(ic == 7))
            em.op("scalar", lambda e, oc=oc, ps=ps: e.activation(
                out=pstg[:, oc, 3:TT + 3], in_=ps[:, 0:TT], func=AF.Identity),
                reads=[ps_b], writes=[pstg_b])
        for oc in range(12):
            if oc < 3:
                cw, cw_b, ci = scq, scq_b, oc
            elif oc < 6:
                cw, cw_b, ci = sck, sck_b, oc - 3
            else:
                cw, cw_b, ci = scv, scv_b, oc - 6
            ca, ca_b = cacc[oc % 2]
            em.op("vector", lambda e, oc=oc, ci=ci, ca=ca, cw=cw: e.tensor_scalar(
                out=ca[:], in0=pstg[:, oc, 3:TT + 3], scalar1=cw[:, ci, 3:4], scalar2=None,
                op0=ALU.mult), reads=[pstg_b, cw_b], writes=[ca_b])
            for tap in range(3):
                eng = "vector"
                em.op(eng, lambda e, oc=oc, ci=ci, tap=tap, ca=ca, cw=cw: e.scalar_tensor_tensor(
                    out=ca[:], in0=pstg[:, oc, tap:TT + tap], scalar=cw[:, ci, tap:tap + 1],
                    in1=ca[:], op0=ALU.mult, op1=ALU.add),
                    reads=[pstg_b, cw_b, ca_b], writes=[ca_b])
            em.op("scalar", lambda e, oc=oc, ca=ca: e.activation(
                out=qk[:, oc, :], in_=ca[:], func=AF.Silu), reads=[ca_b], writes=[qk_b])
        for jj in range(4):
            ps, ps_b = pp[0]
            ps2, ps2_b = pp[1]
            for ic in range(8):
                mm(ps[:, 0:512], xf[:, ic, 128 * jj:128 * (jj + 1)], swg[:, ic, 0:512],
                   [xf_b, swg_b], ps_b, start=(ic == 0), stop=(ic == 7))
            for ic in range(8):
                mm(ps2[:, 0:256], xf[:, ic, 128 * jj:128 * (jj + 1)], swg[:, ic, 512:768],
                   [xf_b, swg_b], ps2_b, start=(ic == 0), stop=(ic == 7))
            em.op("scalar", lambda e, jj=jj, ps=ps: e.activation(
                out=gt_t[:, jj, 0:512], in_=ps[:, 0:512], func=AF.Silu),
                reads=[ps_b], writes=[gt_b])
            em.op("scalar", lambda e, jj=jj, ps2=ps2: e.activation(
                out=gt_t[:, jj, 512:768], in_=ps2[:, 0:256], func=AF.Silu),
                reads=[ps2_b], writes=[gt_b])
        for jj in range(4):
            c = mt * 4 + jj
            slot = c % nring
            bc, bc_b = bcol[slot]
            bec, bec_b = becc[slot]
            ec, ec_b = ecum[slot]
            cm, cm_b = cumc[slot]
            ct, ct_b = cumt[slot]
            eb, eb_b = eccb[slot]
            ef, ef_b = efac[slot]
            for ic in range(8):
                mm(psF[:, 128:136], xf[:, ic, 128 * jj:128 * (jj + 1)], swba[:, ic, 0:8],
                   [xf_b, swba_b], psF_b, start=(ic == 0), stop=False)
            mm(psF[:, 128:136], onesrow[:], dtb_bf[:], [onesrow_b, dtb_bf_b],
               psF_b, start=False, stop=True)
            if getattr(cx, "debug", False) and c == 0:
                dpre, dpre_b = cx.sb("dpre", (128, 12), F32)
                em.op("vector", lambda e: e.tensor_copy(out=dpre[:, 0:8], in_=psF[:, 128:136]),
                      reads=[psF_b], writes=[dpre_b])
                cx.dpre = (dpre, dpre_b)
            em.op("scalar", lambda e: e.activation(
                out=mcol[:, 0:3], in_=psF[:, 128:131], func=AF.Exp, scale=-1.0),
                reads=[psF_b], writes=[mcol_b])
            if getattr(cx, "debug", False) and c == 0:
                dpre, dpre_b = cx.dpre
                em.op("vector", lambda e: e.tensor_copy(out=dpre[:, 8:11], in_=mcol[:, 0:3]),
                      reads=[mcol_b], writes=[dpre_b])
            em.op("vector", lambda e: e.tensor_scalar(
                out=mcol[:, 0:3], in0=mcol[:, 0:3], scalar1=1.0, scalar2=None, op0=ALU.add),
                reads=[mcol_b], writes=[mcol_b])
            em.op("vector", lambda e, bc=bc: e.reciprocal(out=bc[:], in_=mcol[:, 0:3]),
                  reads=[mcol_b], writes=[bc_b])
            em.op("scalar", lambda e: e.activation(
                out=mcol[:, 0:3], in_=psF[:, 131:134], func=AF.Abs),
                reads=[psF_b], writes=[mcol_b])
            em.op("scalar", lambda e: e.activation(
                out=mcol[:, 0:3], in_=mcol[:, 0:3], func=AF.Exp, scale=-1.0),
                reads=[mcol_b], writes=[mcol_b])
            em.op("scalar", lambda e: e.activation(
                out=mcol[:, 0:3], in_=mcol[:, 0:3], func=AF.Ln, bias=cb[:, 3:4]),
                reads=[mcol_b, cb_b], writes=[mcol_b])
            em.op("vector", lambda e: e.tensor_scalar(
                out=mcol[:, 3:6], in0=psF[:, 131:134], scalar1=0.0, scalar2=None,
                op0=ALU.max), reads=[psF_b], writes=[mcol_b])
            em.op("vector", lambda e: e.tensor_tensor(
                out=mcol[:, 0:3], in0=mcol[:, 0:3], in1=mcol[:, 3:6], op=ALU.add),
                reads=[mcol_b], writes=[mcol_b])
            em.op("vector", lambda e: e.tensor_tensor(
                out=mcol[:, 0:3], in0=mcol[:, 0:3], in1=nalog_bc[:], op=ALU.mult),
                reads=[mcol_b, nalog_bc_b], writes=[mcol_b])
            mm(psF[:, 128:131], triu[:], mcol[:, 0:3], [triu_b, mcol_b], psF_b)
            em.op("vector", lambda e, cm=cm: e.tensor_copy(out=cm[:], in_=psF[:, 128:131]),
                  reads=[psF_b], writes=[cm_b])
            em.op("scalar", lambda e, ec=ec, cm=cm: e.activation(
                out=ec[:], in_=cm[:], func=AF.Exp), reads=[cm_b], writes=[ec_b])
            tr(psF[0:3, 0:128], cm[:, 0:3], [cm_b], psF_b, idt=identf, idb=identf_b)
            em.op("vector", lambda e, ct=ct: e.tensor_copy(out=ct[0:3, :], in_=psF[0:3, 0:128]),
                  reads=[psF_b], writes=[ct_b])
            mm(psF[:, 128:131], sel_last[:], cm[:], [sel_last_b, cm_b], psF_b)
            em.op("scalar", lambda e, eb=eb: e.activation(
                out=eb[:], in_=psF[:, 128:131], func=AF.Exp), reads=[psF_b], writes=[eb_b])
            em.op("vector", lambda e, cm=cm: e.tensor_tensor(
                out=mcol[:, 3:6], in0=psF[:, 128:131], in1=cm[:], op=ALU.subtract),
                reads=[psF_b, cm_b], writes=[mcol_b])
            em.op("scalar", lambda e, ef=ef: e.activation(
                out=ef[:], in_=mcol[:, 3:6], func=AF.Exp), reads=[mcol_b], writes=[ef_b])
            em.op("vector", lambda e, bec=bec, bc=bc, ec=ec: e.tensor_tensor(
                out=bec[:], in0=bc[:], in1=ec[:], op=ALU.mult),
                reads=[bc_b, ec_b], writes=[bec_b])
            phase_a(c, slot, jj, qk, qk_b, gt_t, gt_b)
            if getattr(cx, "debug", False) and c == 0:
                dumps = [
                    ("xf0", xf[:, 0, 0:128], xf_b, BF16, 128),
                    ("q0", qk[:, 0, 0:128], qk_b, BF16, 128),
                    ("k0n", qk[:, 3, 0:128], qk_b, BF16, 128),
                    ("v0", qk[:, 6, 0:128], qk_b, BF16, 128),
                    ("gate0", gt_t[:, 0, 0:256], gt_b, BF16, 256),
                    ("bc", bc[:], bc_b, F32, 3),
                    ("cum", cm[:], cm_b, F32, 3),
                    ("ecum", ec[:], ec_b, F32, 3),
                    ("eb", eb[:], eb_b, F32, 3),
                    ("ef", ef[:], ef_b, F32, 3),
                    ("am0", amat[0][0][:], amat[0][1], BF16, 128),
                    ("zt0", zt[0][0][:], zt[0][1], BF16, 128),
                    ("u10", u1r[slot][0][0][:], u1r[slot][0][1], BF16, 256),
                    ("w1t0", w1t[slot][0][0][:], w1t[slot][0][1], BF16, 128),
                    ("mt0", mtr[slot][0][0][:], mtr[slot][0][1], BF16, 128),
                    ("kh0", khr[slot][0][0][:], khr[slot][0][1], BF16, 128),
                    ("sq0", sqr[slot][0][0][:], sqr[slot][0][1], F32, 2),
                    ("dpre", cx.dpre[0][:], cx.dpre[1], F32, 12),
                    ("ktm0", ktm[:], ktm_b, BF16, 128),
                    ("vtm0", vtm[0][0][:], vtm[0][1], BF16, 256),
                    ("kbtm0", kbtm[0][0][:], kbtm[0][1], BF16, 128),
                ]
                dstg, dstg_b = cx.sb("dstg", (128, 256), F32)
                colofs = 0
                cx.dbg_layout = []
                for name, ap, rbuf, dt, w in dumps:
                    em.op("vector", lambda e, ap=ap, w=w: e.tensor_copy(
                        out=dstg[:, 0:w], in_=ap), reads=[rbuf], writes=[dstg_b])
                    em.dma("scalar", lambda e, w=w, co=colofs: e.dma_start(
                        out=cx.dbg_out[0:128, co:co + w], in_=dstg[:, 0:w]),
                        reads=[dstg_b], writes=[])
                    cx.dbg_layout.append((name, colofs, w))
                    colofs += w
            phase_b(c, slot, jj, qk, qk_b, gt_t, gt_b)
            if getattr(cx, "debug", False) and c == 0:
                dumps2 = [
                    ("S1", Sb[0][0][:], Sb[0][1], BF16, 256),
                    ("u0fin", u1r[slot][0][0][:], u1r[slot][0][1], BF16, 256),
                ]
                for name, ap, rbuf, dt, w in dumps2:
                    em.op("vector", lambda e, ap=ap, w=w: e.tensor_copy(
                        out=dstg[:, 0:w], in_=ap), reads=[rbuf], writes=[dstg_b])
                    em.dma("scalar", lambda e, w=w, co=colofs: e.dma_start(
                        out=cx.dbg_out[0:128, co:co + w], in_=dstg[:, 0:w]),
                        reads=[dstg_b], writes=[])
                    cx.dbg_layout.append((name, colofs, w))
                    colofs += w

    for mt in range(NT):
        do_mt(mt, *xst[mt % 2], *xbt[mt % 2], *xfm[mt % 2], *qkv[mt % 2], *gat[mt % 2])



TP2 = 2048


def build_ffn_program(nt2=None, debug=False):
    nc = bass.Bass("TRN2", target_bir_lowering=False)
    em = Emitter()
    with ExitStack() as st:
        cx = Ctx(nc, em, st)
        x_in = nc.dram_tensor("x", [TP2, D], F32, kind="ExternalInput").ap()
        og_in = nc.dram_tensor("og", [H * DV, TP2], F32, kind="ExternalInput").ap()
        wo_in = nc.dram_tensor("wo", [H * DV, D], F32, kind="ExternalInput").ap()
        nw_in = nc.dram_tensor("nw", [H * DV, 1], F32, kind="ExternalInput").ap()
        wfc_in = nc.dram_tensor("wfc", [FH, D], F32, kind="ExternalInput").ap()
        wpr_in = nc.dram_tensor("wpr", [D, FH], F32, kind="ExternalInput").ap()
        u_in = nc.dram_tensor("lru", [FH, RK], F32, kind="ExternalInput").ap()
        v_in = nc.dram_tensor("lrv", [RK, D], F32, kind="ExternalInput").ap()
        fb_in = nc.dram_tensor("lrfb", [RK, RK], F32, kind="ExternalInput").ap()
        gate_in = nc.dram_tensor("lrgate", [1, 1], F32, kind="ExternalInput").ap()
        y_out = nc.dram_tensor("y", [TP2, D], F32, kind="ExternalOutput").ap()
        if debug:
            cx.dbgy = nc.dram_tensor("dbgy", [TP2, D], F32, kind="ExternalOutput").ap()
            cx.dbgr2 = nc.dram_tensor("dbgr2", [FH, TT], F32, kind="ExternalOutput").ap()
        wfc_sc = nc.dram_tensor("wfc_scratch", [8, 32, 128, 128], BF16, kind="Internal").ap()
        wpr_sc = nc.dram_tensor("wpr_scratch", [FH, D], BF16, kind="Internal").ap()

        cc = load_constants(cx, em)
        cx.wfc_sc_b = Buf("wfc_sc")
        cx.wpr_sc_b = Buf("wpr_sc")
        build_ffn_body(cx, em, cc, x_in, og_in, wo_in, nw_in, wfc_in, wpr_in,
                       u_in, v_in, fb_in, gate_in, y_out, wfc_sc, wpr_sc, nt2=nt2)
        sems = {}
        with ExitStack() as st2:
            for s in STREAMS:
                for k in ("c", "d"):
                    sems[(s, k)] = st2.enter_context(nc.semaphore(f"sem_{s}_{k}"))
            block = st2.enter_context(nc.Block())
            em.emit(nc, block, sems)
    return nc


def build_ffn_body(cx, em, cc, x_in, og_in, wo_in, nw_in, wfc_in, wpr_in,
                   u_in, v_in, fb_in, gate_in, y_out, wfc_sc, wpr_sc, nt2=None):
    nc = cx.nc
    ident, ident_b = cc["c_ident"]
    identf, identf_b = cc["c_ident_f"]
    cb, cb_b = cc["cbias"]

    def mm(out_ap, lhsT_ap, rhs_ap, reads, wbuf, start=True, stop=True):
        em.op("tensor", lambda e: e.matmul(out=out_ap, lhsT=lhsT_ap, rhs=rhs_ap,
                                           start=start, stop=stop),
              reads=list(reads), writes=[wbuf])

    def tr(out_ap, in_ap, reads, wbuf, p=128):
        em.op("tensor", lambda e: e.transpose(out=out_ap, in_=in_ap, identity=ident[0:p, 0:p]),
              reads=list(reads) + [ident_b], writes=[wbuf])

    pp = [cx.ps(f"fp{i}", (128, 512), F32) for i in range(4)]
    psT, psT_b = cx.ps("psT", (128, 128), BF16)
    psS, psS_b = cx.ps("psS", (128, 128), F32)

    swo, swo_b = cx.sb("swo", (128, 12, D), BF16)
    stg, stg_b = cx.sb("wstg", (128, D), F32)
    nwc, nwc_b = cx.sb("nwc", (128, 12), F32)
    for i in range(12):
        em.dma("sync", lambda e, i=i: e.dma_start(
            out=nwc[:, i:i + 1], in_=nw_in[128 * i:128 * (i + 1), :]), writes=[nwc_b])
    for i in range(12):
        em.dma("sync", lambda e, i=i: e.dma_start(
            out=stg[:], in_=wo_in[128 * i:128 * (i + 1), :]), writes=[stg_b])
        em.op("scalar", lambda e, i=i: e.activation(
            out=swo[:, i, :], in_=stg[:], func=AF.Identity, scale=nwc[:, i:i + 1]),
            reads=[stg_b, nwc_b], writes=[swo_b])

    svt, svt_b = cx.sb("svt", (128, 8, RK), BF16)
    s16, s16_b = cx.sb("s16", (16, 1024), F32)
    sv16, sv16_b = cx.sb("sv16", (16, 1024), BF16)
    em.dma("sync", lambda e: e.dma_start(out=s16[0:16, 0:1024], in_=v_in), writes=[s16_b])
    em.op("vector", lambda e: e.tensor_copy(out=sv16[:], in_=s16[:]),
          reads=[s16_b], writes=[sv16_b])
    for i in range(8):
        tr(psT[0:128, 0:16], sv16[0:16, 128 * i:128 * (i + 1)], [sv16_b], psT_b, p=16)
        em.op("vector", lambda e, i=i: e.tensor_copy(out=svt[:, i, :], in_=psT[0:128, 0:16]),
              reads=[psT_b], writes=[svt_b])
    sfbt, sfbt_b = cx.sb("sfbt", (16, RK), BF16)
    em.dma("sync", lambda e: e.dma_start(out=s16[0:16, 0:16], in_=fb_in), writes=[s16_b])
    em.op("vector", lambda e: e.tensor_copy(out=sv16[0:16, 0:16], in_=s16[0:16, 0:16]),
          reads=[s16_b], writes=[sv16_b])
    tr(psT[0:16, 0:16], sv16[0:16, 0:16], [sv16_b], psT_b, p=16)
    em.op("vector", lambda e: e.tensor_copy(out=sfbt[:], in_=psT[0:16, 0:16]),
          reads=[psT_b], writes=[sfbt_b])
    sgate, sgate_b = cx.sb("sgate", (1, 1), F32)
    em.dma("sync", lambda e: e.dma_start(out=sgate[:], in_=gate_in), writes=[sgate_b])
    sgcol, sgcol_b = cx.sb("sgcol", (16, 1), F32)
    onesrow_f, onesrow_f_b = cx.sb("onesrow_f", (1, 128), F32)
    em.op("vector", lambda e: e.memset(onesrow_f[:], 1.0), writes=[onesrow_f_b])
    mm(psS[0:16, 0:1], onesrow_f[0:1, 0:16], sgate[:], [onesrow_f_b, sgate_b], psS_b)
    em.op("vector", lambda e: e.tensor_copy(out=sgcol[:], in_=psS[0:16, 0:1]),
          reads=[psS_b], writes=[sgcol_b])
    sut, sut_b = cx.sb("sut", (16, 32, 128), BF16)
    sustg, sustg_b = cx.sb("sustg", (128, RK), F32)
    sustb, sustb_b = cx.sb("sustb", (128, RK), BF16)
    for i in range(32):
        em.dma("sync", lambda e, i=i: e.dma_start(
            out=sustg[:], in_=u_in[128 * i:128 * (i + 1), :]), writes=[sustg_b])
        em.op("vector", lambda e: e.tensor_copy(out=sustb[:], in_=sustg[:]),
              reads=[sustg_b], writes=[sustb_b])
        tr(psT[0:16, 0:128], sustb[:], [sustb_b], psT_b)
        em.op("scalar", lambda e, i=i: e.activation(
            out=sut[:, i, :], in_=psT[0:16, 0:128], func=AF.Identity, scale=sgcol[:]),
            reads=[psT_b, sgcol_b], writes=[sut_b])

    tscr, tscr_b = cx.sb("tscr", (128, D), F32)
    tmsk, tmsk_b = cx.sb("tmsk", (128, D), F32)
    tcol, tcol_b = cx.sb("tcol", (128, 2), F32)
    wrow, wrow_b = cx.sb("wrow", (128, D), BF16)
    wblk, wblk_b = cx.sb("wblk", (128, 128), BF16)
    for i in range(32):
        em.dma("sync", lambda e, i=i: e.dma_start(
            out=tscr[:], in_=wfc_in[128 * i:128 * (i + 1), :]), writes=[tscr_b])
        em.op("scalar", lambda e: e.activation(out=tmsk[:], in_=tscr[:], func=AF.Abs),
              reads=[tscr_b], writes=[tmsk_b])
        nred = D
        while nred > 1:
            h_ = nred // 2
            em.op("vector", lambda e, h_=h_, nred=nred: e.tensor_tensor(
                out=tmsk[:, 0:h_], in0=tmsk[:, 0:h_], in1=tmsk[:, h_:nred], op=ALU.add),
                reads=[tmsk_b], writes=[tmsk_b])
            nred = h_
        em.op("vector", lambda e: e.tensor_scalar(
            out=tcol[:, 0:1], in0=tmsk[:, 0:1], scalar1=1.0 / D, scalar2=1e-5,
            op0=ALU.mult, op1=ALU.max), reads=[tmsk_b], writes=[tcol_b])
        em.op("vector", lambda e: e.tensor_scalar(
            out=tmsk[:], in0=tscr[:], scalar1=2.0, scalar2=tcol[:, 0:1],
            op0=ALU.mult, op1=ALU.is_gt), reads=[tscr_b, tcol_b], writes=[tmsk_b])
        em.op("gpsimd", lambda e: e.tensor_scalar(
            out=tscr[:], in0=tscr[:], scalar1=-2.0, scalar2=tcol[:, 0:1],
            op0=ALU.mult, op1=ALU.is_gt), reads=[tscr_b, tcol_b], writes=[tscr_b])
        em.op("vector", lambda e: e.tensor_tensor(
            out=tmsk[:], in0=tmsk[:], in1=tscr[:], op=ALU.subtract),
            reads=[tmsk_b, tscr_b], writes=[tmsk_b])
        em.op("scalar", lambda e: e.activation(
            out=wrow[:], in_=tmsk[:], func=AF.Identity, scale=tcol[:, 0:1]),
            reads=[tmsk_b, tcol_b], writes=[wrow_b])
        for ic in range(8):
            tr(psT[:], wrow[:, 128 * ic:128 * (ic + 1)], [wrow_b], psT_b)
            em.op("vector", lambda e: e.tensor_copy(out=wblk[:], in_=psT[:]),
                  reads=[psT_b], writes=[wblk_b])
            em.dma("scalar", lambda e, i=i, ic=ic: e.dma_start(
                out=wfc_sc[ic, i, :, :], in_=wblk[:]), reads=[wblk_b],
                writes=[cx.wfc_sc_b])

    wps, wps_b = cx.sb("wps", (128, FH), F32)
    awp, awp_b = cx.sb("awp", (128, FH), F32)
    wprow, wprow_b = cx.sb("wprow", (128, FH), BF16)
    for i in range(8):
        em.dma("sync", lambda e, i=i: e.dma_start(
            out=wps[:], in_=wpr_in[128 * i:128 * (i + 1), :]), writes=[wps_b])
        em.op("scalar", lambda e: e.activation(out=awp[:], in_=wps[:], func=AF.Abs),
              reads=[wps_b], writes=[awp_b])
        nred = FH
        while nred > 1:
            h_ = nred // 2
            em.op("vector", lambda e, h_=h_, nred=nred: e.tensor_tensor(
                out=awp[:, 0:h_], in0=awp[:, 0:h_], in1=awp[:, h_:nred], op=ALU.add),
                reads=[awp_b], writes=[awp_b])
            nred = h_
        em.op("vector", lambda e: e.tensor_scalar(
            out=tcol[:, 1:2], in0=awp[:, 0:1], scalar1=1.0 / FH, scalar2=1e-5,
            op0=ALU.mult, op1=ALU.max), reads=[awp_b], writes=[tcol_b])
        for cc in range(4):
            sl = slice(1024 * cc, 1024 * (cc + 1))
            em.op("vector", lambda e, sl=sl: e.tensor_scalar(
                out=tmsk[:], in0=wps[:, sl], scalar1=2.0, scalar2=tcol[:, 1:2],
                op0=ALU.mult, op1=ALU.is_gt), reads=[wps_b, tcol_b], writes=[tmsk_b])
            em.op("vector", lambda e, sl=sl: e.tensor_scalar(
                out=tscr[:], in0=wps[:, sl], scalar1=-2.0, scalar2=tcol[:, 1:2],
                op0=ALU.mult, op1=ALU.is_gt), reads=[wps_b, tcol_b], writes=[tscr_b])
            em.op("vector", lambda e: e.tensor_tensor(
                out=tmsk[:], in0=tmsk[:], in1=tscr[:], op=ALU.subtract),
                reads=[tmsk_b, tscr_b], writes=[tmsk_b])
            em.op("scalar", lambda e, sl=sl: e.activation(
                out=wprow[:, sl], in_=tmsk[:], func=AF.Identity, scale=tcol[:, 1:2]),
                reads=[tmsk_b, tcol_b], writes=[wprow_b])
        for jc in range(32):
            tr(psT[:], wprow[:, 128 * jc:128 * (jc + 1)], [wprow_b], psT_b)
            em.op("vector", lambda e: e.tensor_copy(out=wblk[:], in_=psT[:]),
                  reads=[psT_b], writes=[wblk_b])
            em.dma("scalar", lambda e, i=i, jc=jc: e.dma_start(
                out=wpr_sc[128 * jc:128 * (jc + 1), 128 * i:128 * (i + 1)],
                in_=wblk[:]), reads=[wblk_b], writes=[cx.wpr_sc_b])

    xti = [cx.sb(f"xti{i}", (128, D), F32) for i in range(2)]
    ogi = [cx.sb(f"ogi{i}", (128, 12, 128), BF16) for i in range(2)]
    ogf = [cx.sb(f"ogf{i}", (128, 12, 128), F32) for i in range(1)]
    yti = [cx.sb(f"yti{i}", (128, 4, D), F32) for i in range(1)]
    x2b, x2b_b = cx.sb("x2b", (128, D), BF16)
    x2f = [cx.sb(f"x2f{i}", (128, 8, TT), BF16) for i in range(1)]
    r2t, r2t_b = cx.sb("r2", (128, 32, TT), BF16)
    c1, c1_b = cx.sb("c1", (16, TT), BF16)
    c2, c2_b = cx.sb("c2", (16, TT), BF16)
    wef = [cx.sb(f"wef{i}", (128, 128), BF16) for i in range(2)]
    wpt = [cx.sb(f"wpt{i}", (128, 512), BF16) for i in range(2)]
    mc2, mc2_b = cx.sb("mc2", (128, 8), F32)
    outt, outt_b = cx.sb("outt", (128, D), F32)

    NT2 = TP2 // TT

    def do_t5(t5, yt, yt_b, xf2, xf2_b):
        for jj in range(4):
            tk = t5 * TT + jj * 128
            xt, xt_b = xti[jj % 2]
            og_t, og_b = ogi[jj % 2]
            ogf_t, ogf_b = ogf[0]
            em.dma("sync", lambda e, tk=tk, xt=xt: e.dma_start(
                out=xt[:], in_=x_in[tk:tk + 128, :]), writes=[xt_b])
            for i in range(12):
                em.dma("gpsimd", lambda e, i=i, tk=tk, ogf_t=ogf_t: e.dma_start(
                    out=ogf_t[:, i, :], in_=og_in[128 * i:128 * (i + 1), tk:tk + 128]),
                    writes=[ogf_b])
            for i in range(12):
                em.op("vector", lambda e, i=i, og_t=og_t, ogf_t=ogf_t: e.tensor_copy(
                    out=og_t[:, i, :], in_=ogf_t[:, i, :]), reads=[ogf_b], writes=[og_b])
            ps, ps_b = pp[0]
            ps2, ps2_b = pp[1]
            for i in range(12):
                mm(ps[:, 0:512], og_t[:, i, :], swo[:, i, 0:512],
                   [og_b, swo_b], ps_b, start=(i == 0), stop=(i == 11))
            for i in range(12):
                mm(ps2[:, 0:512], og_t[:, i, :], swo[:, i, 512:1024],
                   [og_b, swo_b], ps2_b, start=(i == 0), stop=(i == 11))
            em.op("vector", lambda e, jj=jj, ps=ps, yt=yt, xt=xt: e.tensor_tensor(
                out=yt[:, jj, 0:512], in0=ps[:, 0:512], in1=xt[:, 0:512], op=ALU.add),
                reads=[ps_b, xt_b], writes=[yt_b])
            em.op("vector", lambda e, jj=jj, ps2=ps2, yt=yt, xt=xt: e.tensor_tensor(
                out=yt[:, jj, 512:1024], in0=ps2[:, 0:512], in1=xt[:, 512:1024], op=ALU.add),
                reads=[ps2_b, xt_b], writes=[yt_b])
            if getattr(cx, "dbgy", None) is not None:
                em.dma("scalar", lambda e, jj=jj, tk=tk, yt=yt: e.dma_start(
                    out=cx.dbgy[tk:tk + 128, :], in_=yt[:, jj, :]),
                    reads=[yt_b], writes=[])
            em.op("vector", lambda e, jj=jj, yt=yt: e.tensor_reduce(
                out=mc2[:, 0:1], in_=yt[:, jj, :], axis=AX.X, op=ALU.add),
                reads=[yt_b], writes=[mc2_b])
            em.op("scalar", lambda e, jj=jj, yt=yt: e.activation(
                out=tscr[:], in_=yt[:, jj, :], func=AF.Square),
                reads=[yt_b], writes=[tscr_b])
            em.op("vector", lambda e: e.tensor_reduce(
                out=mc2[:, 1:2], in_=tscr[:], axis=AX.X, op=ALU.add),
                reads=[tscr_b], writes=[mc2_b])
            em.op("vector", lambda e: e.tensor_scalar(
                out=mc2[:, 2:3], in0=mc2[:, 0:1], scalar1=1.0 / D, scalar2=None,
                op0=ALU.mult), reads=[mc2_b], writes=[mc2_b])
            em.op("vector", lambda e: e.tensor_tensor(
                out=mc2[:, 3:4], in0=mc2[:, 2:3], in1=mc2[:, 2:3], op=ALU.mult),
                reads=[mc2_b], writes=[mc2_b])
            em.op("vector", lambda e: e.tensor_scalar(
                out=mc2[:, 4:5], in0=mc2[:, 1:2], scalar1=1.0 / D, scalar2=mc2[:, 3:4],
                op0=ALU.mult, op1=ALU.subtract), reads=[mc2_b], writes=[mc2_b])
            em.op("scalar", lambda e: e.activation(
                out=mc2[:, 5:6], in_=mc2[:, 4:5], func=AF.Ln, bias=cb[:, 0:1]),
                reads=[mc2_b, cb_b], writes=[mc2_b])
            em.op("scalar", lambda e: e.activation(
                out=mc2[:, 6:7], in_=mc2[:, 5:6], func=AF.Exp, scale=-0.5),
                reads=[mc2_b], writes=[mc2_b])
            em.op("vector", lambda e, jj=jj, yt=yt: e.tensor_scalar(
                out=x2b[:], in0=yt[:, jj, :], scalar1=mc2[:, 2:3], scalar2=mc2[:, 6:7],
                op0=ALU.subtract, op1=ALU.mult), reads=[yt_b, mc2_b], writes=[x2b_b])
            for ic in range(8):
                tr(psT[:], x2b[:, 128 * ic:128 * (ic + 1)], [x2b_b], psT_b)
                em.op("vector", lambda e, jj=jj, ic=ic, xf2=xf2: e.tensor_copy(
                    out=xf2[:, ic, 128 * jj:128 * (jj + 1)], in_=psT[:]),
                    reads=[psT_b], writes=[xf2_b])
        ps, ps_b = pp[0]
        for ic in range(8):
            mm(ps[0:16, 0:TT], svt[:, ic, :], xf2[:, ic, :], [svt_b, xf2_b], ps_b,
               start=(ic == 0), stop=(ic == 7))
        em.op("vector", lambda e, ps=ps: e.tensor_copy(out=c1[:], in_=ps[0:16, 0:TT]),
              reads=[ps_b], writes=[c1_b])
        mm(ps[0:16, 0:TT], sfbt[:], c1[0:16, :], [sfbt_b, c1_b], ps_b)
        em.op("vector", lambda e, ps=ps: e.tensor_copy(out=c2[:], in_=ps[0:16, 0:TT]),
              reads=[ps_b], writes=[c2_b])
        for jc in range(32):
            ps, ps_b = pp[jc % 2]
            for ic in range(8):
                wf, wf_b = wef[(jc * 8 + ic) % 2]
                em.dma("gpsimd", lambda e, jc=jc, ic=ic, wf=wf: e.dma_start(
                    out=wf[:], in_=wfc_sc[ic, jc, :, :]), reads=[cx.wfc_sc_b],
                    writes=[wf_b])
                mm(ps[:, 0:TT], wf[:], xf2[:, ic, :], [wf_b, xf2_b], ps_b,
                   start=(ic == 0), stop=False)
            mm(ps[:, 0:TT], sut[:, jc, :], c2[0:16, :], [sut_b, c2_b], ps_b,
               start=False, stop=True)
            em.op("vector", lambda e, jc=jc, ps=ps: e.tensor_scalar(
                out=r2t[:, jc, :], in0=ps[:, 0:TT], scalar1=0.0, scalar2=None,
                op0=ALU.max), reads=[ps_b], writes=[r2t_b])
            em.op("scalar", lambda e, jc=jc: e.activation(
                out=r2t[:, jc, :], in_=r2t[:, jc, :], func=AF.Square),
                reads=[r2t_b], writes=[r2t_b])
        if getattr(cx, "dbgr2", None) is not None and t5 == 1:
            r2d, r2d_b = cx.sb("r2d", (128, TT), F32)
            for jc in range(32):
                em.op("vector", lambda e, jc=jc: e.tensor_copy(
                    out=r2d[:], in_=r2t[:, jc, :]), reads=[r2t_b], writes=[r2d_b])
                em.dma("scalar", lambda e, jc=jc: e.dma_start(
                    out=cx.dbgr2[128 * jc:128 * (jc + 1), :], in_=r2d[:]),
                    reads=[r2d_b], writes=[])
        for dh in range(2):
            for jc in range(32):
                wp, wp_b = wpt[jc % 2]
                em.dma("gpsimd", lambda e, jc=jc, dh=dh, wp=wp: e.dma_start(
                    out=wp[:], in_=wpr_sc[128 * jc:128 * (jc + 1), 512 * dh:512 * (dh + 1)]),
                    reads=[cx.wpr_sc_b], writes=[wp_b])
                for jj in range(4):
                    ps, ps_b = pp[jj]
                    mm(ps[:, 0:512], r2t[:, jc, 128 * jj:128 * (jj + 1)], wp[:],
                       [r2t_b, wp_b], ps_b, start=(jc == 0), stop=(jc == 31))
            for jj in range(4):
                ps, ps_b = pp[jj]
                em.op("vector", lambda e, jj=jj, dh=dh, ps=ps, yt=yt: e.tensor_tensor(
                    out=outt[:, 512 * dh:512 * (dh + 1)], in0=ps[:, 0:512],
                    in1=yt[:, jj, 512 * dh:512 * (dh + 1)], op=ALU.add),
                    reads=[ps_b, yt_b], writes=[outt_b])
                tk = t5 * TT + jj * 128
                em.dma("scalar", lambda e, tk=tk, dh=dh: e.dma_start(
                    out=y_out[tk:tk + 128, 512 * dh:512 * (dh + 1)],
                    in_=outt[:, 512 * dh:512 * (dh + 1)]), reads=[outt_b], writes=[])

    for t5 in range(NT2 if nt2 is None else nt2):
        do_t5(t5, *yti[0], *x2f[0])



_ATTN_NC = None
_FFN_NC = None


def kernel(**inputs):
    global _ATTN_NC, _FFN_NC
    f32 = np.float32
    g = lambda n: np.ascontiguousarray(np.asarray(inputs[n], dtype=f32))
    x = g("L0_attn_x")
    Wq, Wk, Wv = g("L0_attn_Wq"), g("L0_attn_Wk"), g("L0_attn_Wv")
    Wb, Wa = g("L0_attn_Wb"), g("L0_attn_Wa")
    A_log, dt_bias = g("L0_attn_A_log"), g("L0_attn_dt_bias")
    conv_q, conv_k, conv_v = g("L0_attn_conv_q"), g("L0_attn_conv_k"), g("L0_attn_conv_v")
    Wg, norm_w, Wo = g("L0_attn_Wg"), g("L0_attn_norm_w"), g("L0_attn_Wo")
    fast_b, w_fc, w_proj = g("L0_ffn_fast_b"), g("L0_ffn_w_fc"), g("L0_ffn_w_proj")
    u, v, gate = g("L0_ffn_u"), g("L0_ffn_v"), g("L0_ffn_gate")

    consts = host_constants()
    if _ATTN_NC is None:
        _ATTN_NC = build_attn_program()
    if _FFN_NC is None:
        _FFN_NC = build_ffn_program()

    in_maps = []
    for c in range(8):
        b, hg = c // 2, c % 2
        hs = slice(384 * hg, 384 * (hg + 1))
        vs = slice(768 * hg, 768 * (hg + 1))
        h3 = slice(3 * hg, 3 * (hg + 1))
        wba = np.concatenate([Wb[:, h3], Wa[:, h3], np.zeros((D, 2), f32)], axis=1)
        aldt = np.concatenate([A_log[h3], dt_bias[h3], np.zeros(2, f32)])[None, :]
        m = dict(x=x[b], wq=Wq[:, hs], wk=Wk[:, hs], wv=Wv[:, vs], wg=Wg[:, vs],
                 wba=wba, convq=conv_q[hs], convk=conv_k[hs], convv=conv_v[vs],
                 aldt=aldt, **consts)
        in_maps.append({k: np.ascontiguousarray(v) for k, v in m.items()})
    res1 = run_bass_kernel_spmd(_ATTN_NC, in_maps, core_ids=list(range(8)))
    og = [res1.results[c]["og"] for c in range(8)]

    nw_rep = np.tile(norm_w, H)[:, None]
    in_maps2 = []
    for c in range(8):
        b, half = c // 2, c % 2
        ts = slice(2048 * half, 2048 * (half + 1))
        og_full = np.concatenate([og[2 * b][:, ts], og[2 * b + 1][:, ts]], axis=0)
        m = dict(x=x[b][ts], og=og_full, wo=Wo, nw=nw_rep, wfc=w_fc, wpr=w_proj,
                 lru=u, lrv=v, lrfb=fast_b, lrgate=gate.reshape(1, 1), **consts)
        in_maps2.append({k: np.ascontiguousarray(v) for k, v in m.items()})
    res2 = run_bass_kernel_spmd(_FFN_NC, in_maps2, core_ids=list(range(8)))
    out = np.empty((B, L, D), np.float32)
    for c in range(8):
        b, half = c // 2, c % 2
        out[b, 2048 * half:2048 * (half + 1), :] = res2.results[c]["y"]
    return out
